# version 04 (final)
# baseline (speedup 1.0000x reference)
import numpy as np
import ml_dtypes
import concourse.bass as bass
import concourse.mybir as mybir
from concourse.bass_utils import run_bass_kernel_spmd

F32 = mybir.dt.float32
BF16 = mybir.dt.bfloat16
AF = mybir.ActivationFunctionType
ALU = mybir.AluOpType
AX = mybir.AxisListType

P = 128
D = 2048
NCHUNK = D // P
LN_EPS = 1e-5

ENGS = ("sync", "tensor", "scalar", "vector", "gpsimd")


class Prog:

    def __init__(self, nc):
        self.nc = nc
        self.q = {e: [] for e in ENGS}
        self.sems = {}
        self.cnt = {}
        self.last_wait = {e: {} for e in ENGS}
        self.engsem = {"tensor": "T", "vector": "V", "scalar": "S",
                       "sync": "sdma", "gpsimd": "gdma"}

    def sem(self, name):
        if name not in self.sems:
            self.sems[name] = self.nc.alloc_semaphore(name)
            self.cnt[name] = 0
        return self.sems[name]

    def wait(self, eng, sem_name, val):
        if val <= 0:
            return
        if self.last_wait[eng].get(sem_name, -1) >= val:
            return
        self.last_wait[eng][sem_name] = val
        h = self.sem(sem_name)
        self.q[eng].append(lambda e, h=h, v=val: e.wait_ge(h, v))

    PARANOID = set()

    def op(self, eng, fn, waits=(), dma=False, inc=True, dsem=None):
        own = self.engsem[eng]
        if eng in Prog.PARANOID:
            for s, v in list(self.cnt.items()):
                self.wait(eng, s, v)
        for tok in waits:
            if tok is None:
                continue
            s, v = tok
            self.wait(eng, s, v)
        if not inc:
            self.q[eng].append(fn)
            return None
        if dma:
            assert dsem is not None, "DMA ops need a per-buffer dsem"
            sname = dsem
            k = 16
        else:
            sname = own
            k = 1
        h = self.sem(sname)
        self.cnt[sname] += k
        tgt = self.cnt[sname]
        self.q[eng].append(lambda e, fn=fn, h=h, k=k: fn(e).then_inc(h, k))
        return (sname, tgt)

    def barrier(self):
        snapshot = dict(self.cnt)
        for e in ENGS:
            for s, v in snapshot.items():
                self.wait(e, s, v)

    def build(self):
        with self.nc.Block() as block:
            for e in ENGS:
                items = self.q[e]
                if not items:
                    continue
                getattr(block, e)(lambda eng, items=items: [f(eng) for f in items])


class Ctx:
    def __init__(self, nc):
        self.nc = nc
        self.prog = Prog(nc)
        self.n = 0

    def _nm(self, base):
        self.n += 1
        return f"{base}_{self.n}"

    def sb(self, name, shape, dt):
        return self.nc.alloc_sbuf_tensor(self._nm(name), shape, dt)

    def ps(self, name, shape, dt=F32):
        return self.nc.alloc_psum_tensor(self._nm(name), shape, dt)

    def const_col(self, value):
        if not hasattr(self, "_consts"):
            self._consts = {}
        if value not in self._consts:
            t = self.sb("cc", [P, 1], F32)
            tok = self.prog.op("vector", lambda e, t=t, v=value:
                               e.memset(t[:], v))
            self._consts[value] = (t, tok)
        return self._consts[value]


def later(*toks):
    best = {}
    for t in toks:
        if t is None:
            continue
        s, v = t
        if best.get(s, -1) < v:
            best[s] = v
    return [(s, v) for s, v in best.items()]


def make_ln_bufs(ctx, fm_width):
    return {
        "sumc": ctx.sb("ln_sum", [P, 1], F32),
        "mean": ctx.sb("ln_mean", [P, 1], F32),
        "xc": ctx.sb("ln_xc", [P, D], F32),
        "sq": ctx.sb("ln_sq", [P, D], F32),
        "var": ctx.sb("ln_var", [P, 1], F32),
        "rstd": ctx.sb("ln_rstd", [P, 1], F32),
        "xhat_tm": ctx.sb("ln_xhat", [P, D], BF16),
        "tbanks": [ctx.ps("ln_tb", [P, 4 * P], BF16) for _ in range(2)],
        "tb_free": [None, None],
        "ident": None,
        "W": fm_width,
        "xsem": "xk",
        "prev_mean_reader": None,
        "prev_sumc_reader": None,
        "prev_xc_reader": None,
        "prev_xc_reader_v": None,
        "prev_var_reader": None,
        "prev_rstd_reader": None,
        "prev_xhat_reader": None,
    }


def emit_ln_xhat(ctx, x_dram, tile_idx, b, out_fm, out_tok_off, x_keep,
                 x_write_guards=()):
    p = ctx.prog
    W = b["W"]

    t_ld = p.op("sync", lambda e, x=x_keep, s=x_dram, i=tile_idx:
                e.dma_start(out=x[:], in_=s[i * P:(i + 1) * P, :]),
                waits=later(*x_write_guards), dma=True, dsem=b["xsem"])
    t_sum = p.op("vector", lambda e, x=x_keep, o=b["sumc"]:
                 e.reduce_sum(o[:], x[:], axis=AX.X),
                 waits=later(t_ld, b["prev_sumc_reader"]))
    t_mean = p.op("scalar", lambda e, i=b["sumc"], o=b["mean"]:
                  e.mul(o[:], i[:], 1.0 / D),
                  waits=later(t_sum, b["prev_mean_reader"]))
    b["prev_sumc_reader"] = t_mean
    t_xc = p.op("vector", lambda e, x=x_keep, m=b["mean"], o=b["xc"]:
                e.tensor_scalar(o[:], x[:], m[:], None, ALU.subtract),
                waits=later(t_mean, b["prev_xc_reader"], b["prev_xc_reader_v"]))
    b["prev_mean_reader"] = t_xc
    t_sq = p.op("scalar", lambda e, i=b["xc"], o=b["sq"], a=b["var"]:
                e.activation(o[:], i[:], AF.Square, accum_out=a[:]),
                waits=later(t_xc, b["prev_var_reader"]))
    b["prev_xc_reader"] = t_sq
    eps_col, eps_tok = ctx.const_col(LN_EPS)
    t_sd = p.op("scalar", lambda e, i=b["var"], o=b["rstd"], bc=eps_col:
                e.activation(o[:], i[:], AF.Sqrt, bias=bc[:], scale=1.0 / D),
                waits=later(t_sq, b["prev_rstd_reader"], eps_tok))
    b["prev_var_reader"] = t_sd
    t_r = p.op("vector", lambda e, i=b["rstd"]: e.reciprocal(i[:], i[:]),
               waits=later(t_sd))
    t_xh = p.op("vector", lambda e, x=b["xc"], r=b["rstd"], o=b["xhat_tm"]:
                e.tensor_scalar(o[:], x[:], r[:], None, ALU.mult),
                waits=later(t_r, b["prev_xhat_reader"]))
    b["prev_rstd_reader"] = t_xh
    b["prev_xc_reader_v"] = t_xh

    last_copy = None
    last_mm = None
    for rnd in range(4):
        bank = b["tbanks"][rnd % 2]
        tmm = None
        for j in range(4):
            c = rnd * 4 + j
            wl = later(t_xh, b["tb_free"][rnd % 2]) if j == 0 else ()
            tmm = p.op("tensor", lambda e, bk=bank, j=j, c=c, xh=b["xhat_tm"],
                       idn=b["ident"]:
                       e.transpose(bk[:, j * P:(j + 1) * P],
                                   xh[:, c * P:(c + 1) * P], idn[:]),
                       waits=wl)
        c0 = rnd * 4
        dst = out_fm[:, :].rearrange("p (c w) -> p c w", w=W)[
            :, c0:c0 + 4, out_tok_off:out_tok_off + P]
        srcv = bank[:, 0:4 * P].rearrange("p (c w) -> p c w", w=P)
        tcp = p.op("vector", lambda e, s=srcv, d=dst: e.tensor_copy(d, s),
                   waits=later(tmm))
        b["tb_free"][rnd % 2] = tcp
        last_copy = tcp
        last_mm = tmm
    b["prev_xhat_reader"] = last_mm
    return t_ld, last_copy


def emit_const_load(ctx, dram_ap, shape, dt, name):
    t = ctx.sb(name, shape, dt)
    eng = "gpsimd" if dt == BF16 else "sync"
    tok = ctx.prog.op(eng, lambda e, t=t, s=dram_ap: e.dma_start(out=t[:], in_=s),
                      dma=True, dsem="cst")
    return t, tok


def build_ffn(T):
    INTER = 8192
    NI = INTER // P
    G = 512
    ngroups = T // G

    nc = bass.Bass("TRN2", target_bir_lowering=False)
    x_d = nc.dram_tensor("x", [T, D], F32, kind="ExternalInput").ap()
    wg_d = nc.dram_tensor("wg", [NI, P, NCHUNK, P], F32, kind="ExternalInput").ap()
    wu_d = nc.dram_tensor("wu", [NI, P, NCHUNK, P], F32, kind="ExternalInput").ap()
    wd_d = nc.dram_tensor("wd", [INTER, D], F32, kind="ExternalInput").ap()
    id_d = nc.dram_tensor("ident", [P, P], F32, kind="ExternalInput").ap()
    y_d = nc.dram_tensor("y", [T, D], F32, kind="ExternalOutput").ap()

    ctx = Ctx(nc)
    prog = ctx.prog

    ident, t_id = emit_const_load(ctx, id_d, [P, P], BF16, "ident")
    lnb = make_ln_bufs(ctx, G)
    lnb["ident"] = ident
    lnb["prev_xhat_reader"] = t_id

    xhat_fm = ctx.sb("xhat_fm", [P, NCHUNK * G], BF16)
    x_keep = [ctx.sb("xkeep", [P, D], F32) for _ in range(4)]
    h_fm = ctx.sb("h_fm", [P, NI * G], BF16)
    hgate = [ctx.sb("hgate", [P, G], F32) for _ in range(2)]
    wgt = [ctx.sb("wgt", [P, NCHUNK * P], BF16) for _ in range(2)]
    wut = [ctx.sb("wut", [P, NCHUNK * P], BF16) for _ in range(2)]
    wdt = [ctx.sb("wdt", [P, G], BF16) for _ in range(2)]
    ysb = [ctx.sb("ysb", [P, G], F32) for _ in range(2)]

    pg = ctx.ps("pgate", [P, G])
    pu = ctx.ps("pup", [P, G])
    pdn = [ctx.ps("pdn", [P, G]) for _ in range(4)]

    xhat_busy = None
    h_busy = None
    x_busy = [None] * 4
    y_busy = [None, None]
    pg_busy = None
    pu_busy = None
    hg_busy = [None, None]
    pdn_busy = [None] * 4
    wg_busy = [None, None]
    wu_busy = [None, None]
    wd_busy = [None, None]

    for g in range(ngroups):
        t_fm = None
        for t in range(4):
            guards = [x_busy[t]]
            if t == 0:
                guards.append(xhat_busy)
            lnb["xsem"] = f"xk{t}"
            _, t_fm = emit_ln_xhat(ctx, x_d, g * 4 + t, lnb, xhat_fm, t * P,
                                   x_keep[t], x_write_guards=guards)
        xhat_ready = t_fm

        t_h = None
        for i in range(NI):
            bw = i % 2
            tg = prog.op("gpsimd", lambda e, d=wgt[bw], s=wg_d[i]:
                         e.dma_start(out=d[:], in_=s.rearrange("p c o -> p (c o)")),
                         waits=later(wg_busy[bw]), dma=True, dsem=f"wg{bw}")
            tu = prog.op("gpsimd", lambda e, d=wut[bw], s=wu_d[i]:
                         e.dma_start(out=d[:], in_=s.rearrange("p c o -> p (c o)")),
                         waits=later(wu_busy[bw]), dma=True, dsem=f"wu{bw}")
            tmmg = tmmu = None
            for c in range(NCHUNK):
                wl = later(tg, xhat_ready, pg_busy) if c == 0 else ()
                tmmg = prog.op("tensor", lambda e, w=wgt[bw], c=c, o=pg:
                               e.matmul(o[:], w[:, c * P:(c + 1) * P],
                                        xhat_fm[:, c * G:(c + 1) * G],
                                        start=(c == 0), stop=(c == NCHUNK - 1)),
                               waits=wl)
            for c in range(NCHUNK):
                wl = later(tu, pu_busy) if c == 0 else ()
                tmmu = prog.op("tensor", lambda e, w=wut[bw], c=c, o=pu:
                               e.matmul(o[:], w[:, c * P:(c + 1) * P],
                                        xhat_fm[:, c * G:(c + 1) * G],
                                        start=(c == 0), stop=(c == NCHUNK - 1)),
                               waits=wl)
            wg_busy[bw] = tmmg
            wu_busy[bw] = tmmu
            hb = i % 2
            t_silu = prog.op("scalar", lambda e, o=hgate[hb], i_=pg:
                             e.activation(o[:], i_[:], AF.Silu),
                             waits=later(tmmg, hg_busy[hb]))
            pg_busy = t_silu
            t_h = prog.op("vector", lambda e, hg=hgate[hb], up=pu, o=h_fm, i=i:
                          e.tensor_tensor(o[:, i * G:(i + 1) * G], hg[:], up[:],
                                          ALU.mult),
                          waits=later(t_silu, tmmu, h_busy if i == 0 else None))
            hg_busy[hb] = t_h
            pu_busy = t_h
        h_ready = t_h
        xhat_busy = wu_busy[(NI - 1) % 2]

        last_mm = None
        for n in range(4):
            for i in range(NI):
                bw = i % 2
                src_d = wd_d[i * P:(i + 1) * P, n * G:(n + 1) * G]
                td = prog.op("gpsimd", lambda e, d=wdt[bw], s=src_d:
                             e.dma_start(out=d[:], in_=s),
                             waits=later(wd_busy[bw]), dma=True, dsem=f"wd{bw}")
                for t in range(4):
                    wl = ()
                    if t == 0:
                        wl = later(td, h_ready,
                                   *(pdn_busy if i == 0 else [None]))
                    last_mm = prog.op("tensor",
                                      lambda e, w=wdt[bw], t=t, i=i, o=pdn[t]:
                                      e.matmul(o[:],
                                               h_fm[:, i * G + t * P:i * G + (t + 1) * P],
                                               w[:], start=(i == 0),
                                               stop=(i == NI - 1)),
                                      waits=wl)
                wd_busy[bw] = last_mm
            for t in range(4):
                k = (n * 4 + t) % 2
                yb = ysb[k]
                tev = prog.op("vector", lambda e, o=yb, n=n, ps=pdn[t], x=x_keep[t]:
                              e.tensor_tensor(o[:], ps[:],
                                              x[:, n * G:(n + 1) * G], ALU.add),
                              waits=later(last_mm, y_busy[k]))
                pdn_busy[t] = tev
                x_busy[t] = tev
                tst = prog.op("sync", lambda e, yb=yb, g=g, t=t, n=n:
                              e.dma_start(out=y_d[(g * 4 + t) * P:(g * 4 + t + 1) * P,
                                                  n * G:(n + 1) * G], in_=yb[:]),
                              waits=later(tev), dma=True, dsem=f"ys{k}")
                y_busy[k] = tst
        h_busy = last_mm

    for s, v in list(prog.cnt.items()):
        prog.wait("sync", s, v)
    prog.build()
    return nc


def emit_out_proj(ctx, act_scr, w_dram, x_dram, y_dram, T, NC, banks, bo_bc=None,
                  bo_tok=None, x_sb=None, act_sb=None, wt=None, ysb=None):
    p = ctx.prog
    if act_sb is None:
        act_sb = [ctx.sb("op_act", [P, NC * P], BF16) for _ in range(4)]
    if x_sb is None:
        x_sb = [ctx.sb("op_x", [P, D], F32) for _ in range(4)]
    if wt is None:
        wt = [ctx.sb("op_w", [P, 512], BF16) for _ in range(2)]
    if ysb is None:
        ysb = [ctx.sb("op_y", [P, 512], F32) for _ in range(2)]
    act_busy = [None] * 4
    x_busy = [None] * 4
    w_busy = [None, None]
    y_busy = [None, None]
    bank_busy = [None] * 4
    for gg in range(T // 512):
        a_toks = []
        for t in range(4):
            tile = gg * 4 + t
            a_toks.append(p.op("sync", lambda e, d=act_sb[t], s=act_scr, i=tile:
                               e.dma_start(out=d[:], in_=s[i]),
                               waits=later(act_busy[t]), dma=True, dsem=f"opa{t}"))
            p.op("sync", lambda e, d=x_sb[t], s=x_dram, i=tile:
                 e.dma_start(out=d[:], in_=s[i * P:(i + 1) * P, :]),
                 waits=later(x_busy[t]), dma=True, dsem=f"opx{t}")
            x_busy[t] = (f"opx{t}", p.cnt[f"opx{t}"])
        for n in range(4):
            last_mm = None
            for c in range(NC):
                bw = c % 2
                tw = p.op("gpsimd", lambda e, d=wt[bw], s=w_dram, c=c, n=n:
                          e.dma_start(out=d[:], in_=s[c * P:(c + 1) * P,
                                                      n * 512:(n + 1) * 512]),
                          waits=later(w_busy[bw]), dma=True, dsem=f"opw{bw}")
                for t in range(4):
                    wl = ()
                    if t == 0:
                        wl = later(tw, *a_toks, *(bank_busy if c == 0 else []))
                    last_mm = p.op("tensor",
                                   lambda e, w=wt[bw], t=t, c=c, o=banks[t]:
                                   e.matmul(o[:], act_sb[t][:, c * P:(c + 1) * P],
                                            w[:], start=(c == 0),
                                            stop=(c == NC - 1)), waits=wl)
                w_busy[bw] = last_mm
            for t in range(4):
                k = (n * 4 + t) % 2
                tev = p.op("vector", lambda e, o=ysb[k], ps=banks[t], x=x_sb[t], n=n:
                           e.tensor_tensor(o[:], ps[:], x[:, n * 512:(n + 1) * 512],
                                           ALU.add),
                           waits=later(last_mm, y_busy[k], x_busy[t]))
                if bo_bc is not None:
                    tev = p.op("vector", lambda e, o=ysb[k], b=bo_bc, n=n:
                               e.tensor_tensor(o[:], o[:], b[:, n * 512:(n + 1) * 512],
                                               ALU.add), waits=later(tev, bo_tok))
                bank_busy[t] = tev
                if n == 3:
                    x_busy[t] = tev
                    act_busy[t] = tev
                tst = p.op("sync", lambda e, yb=ysb[k], gg=gg, t=t, n=n:
                           e.dma_start(out=y_dram[(gg * 4 + t) * P:(gg * 4 + t + 1) * P,
                                                  n * 512:(n + 1) * 512], in_=yb[:]),
                           waits=later(tev), dma=True, dsem=f"opy{k}")
                y_busy[k] = tst


def build_attn(T, H, G, dh, has_bias):
    HD = H * dh
    GD = G * dh
    NMQ = HD // P
    NMK = GD // P
    NC = HD // P
    hpc = P // dh
    ktiles = T // P
    ngroups = T // 512
    s = float(dh) ** -0.5

    nc = bass.Bass("TRN2", target_bir_lowering=False)
    x_d = nc.dram_tensor("x", [T, D], F32, kind="ExternalInput").ap()
    wq_d = nc.dram_tensor("wq", [D, HD], F32, kind="ExternalInput").ap()
    wk_d = nc.dram_tensor("wk", [D, GD], F32, kind="ExternalInput").ap()
    wv_d = nc.dram_tensor("wv", [D, GD], F32, kind="ExternalInput").ap()
    wo_d = nc.dram_tensor("wo", [HD, D], F32, kind="ExternalInput").ap()
    id_d = nc.dram_tensor("ident", [P, P], F32, kind="ExternalInput").ap()
    sel_d = nc.dram_tensor("sel", [P, hpc], F32, kind="ExternalInput").ap()
    dm_d = nc.dram_tensor("dmask", [4, P, 512], F32, kind="ExternalInput").ap()
    if has_bias:
        bq_d = nc.dram_tensor("bq2", [P, NMQ], F32, kind="ExternalInput").ap()
        bk_d = nc.dram_tensor("bk2", [P, NMK], F32, kind="ExternalInput").ap()
        bv_d = nc.dram_tensor("bv_row", [1, GD], F32, kind="ExternalInput").ap()
        bo_d = nc.dram_tensor("bo_row", [1, D], F32, kind="ExternalInput").ap()
    y_d = nc.dram_tensor("y", [T, D], F32, kind="ExternalOutput").ap()

    q_scr = nc.dram_tensor("q_scr", [H, dh, T], BF16, kind="Internal").ap()
    k_scr = nc.dram_tensor("k_scr", [G, dh, T], BF16, kind="Internal").ap()
    v_scr = nc.dram_tensor("v_scr", [G, P, ktiles * dh], BF16, kind="Internal").ap()
    a_scr = nc.dram_tensor("a_scr", [ktiles, P, NC * P], BF16, kind="Internal").ap()

    ctx = Ctx(nc)
    prog = ctx.prog

    ident, t_id = emit_const_load(ctx, id_d, [P, P], BF16, "ident")
    sel_sb, t_sel = emit_const_load(ctx, sel_d, [P, hpc], BF16, "sel")
    dm_sb = ctx.sb("dmask", [P, 4 * 512], BF16)
    t_dm = prog.op("gpsimd", lambda e, d=dm_sb, s=dm_d:
                   e.dma_start(out=d[:].rearrange("p (r w) -> p r w", w=512), in_=s.rearrange("r p w -> p r w")),
                   dma=True, dsem="cst2")
    lnb = make_ln_bufs(ctx, 512)
    lnb["ident"] = ident
    lnb["prev_xhat_reader"] = t_id

    ones1 = ctx.sb("ones1", [1, P], F32)
    t_o1 = prog.op("vector", lambda e, t=ones1: e.memset(t[:], 1.0))
    negones1 = ctx.sb("neg1", [1, P], F32)
    t_n1 = prog.op("vector", lambda e, t=negones1: e.memset(t[:], -1.0))
    onescol_bf = ctx.sb("onescol", [P, 1], BF16)
    t_oc = prog.op("vector", lambda e, t=onescol_bf: e.memset(t[:], 1.0))
    mqrow = ctx.sb("mqrow", [1, H], F32)
    mkrow = ctx.sb("mkrow", [1, G], F32)
    mq_tok = prog.op("vector", lambda e, t=mqrow: e.memset(t[:], 0.0))
    mk_tok = prog.op("vector", lambda e, t=mkrow: e.memset(t[:], 0.0))

    xhat_fm = ctx.sb("xhat_fm", [P, NCHUNK * 512], BF16)
    xbufs = [ctx.sb("xk", [P, D], F32) for _ in range(4)]
    wtile = [ctx.sb("wtile", [P, 512], BF16) for _ in range(2)]
    qsb = [ctx.sb("qsb", [P, 512], BF16) for _ in range(2)]
    q2sb = ctx.sb("q2sb", [P, 512], BF16)
    nrow = ctx.sb("nrow", [1, 1], F32)

    banks = [ctx.ps("pa", [P, 512]) for _ in range(4)]
    pb = ctx.ps("pb", [P, 512])
    pnorm = ctx.ps("pnorm", [1, 512])

    state = {
        "w_busy": [None, None],
        "q_busy": [[], []],
        "q2_busy": None,
        "pnorm_busy": None,
        "nrow_busy": None,
        "bank_busy": [None] * 4,
        "pb_busy": None,
    }

    if has_bias:
        bq_sb, t_bq = emit_const_load(ctx, bq_d, [P, NMQ], F32, "bq")
        bk_sb, t_bk = emit_const_load(ctx, bk_d, [P, NMK], F32, "bk")
        bv_row_sb, t_bvr = emit_const_load(ctx, bv_d, [1, GD], F32, "bvr")
        bo_row_sb, t_bor = emit_const_load(ctx, bo_d, [1, D], F32, "bor")
        bv_bc = ctx.sb("bv_bc", [P, GD], F32)
        bo_bc = ctx.sb("bo_bc", [P, D], F32)
        tprev = None
        for dst, row_sb, row_tok, width in ((bv_bc, bv_row_sb, t_bvr, GD),
                                            (bo_bc, bo_row_sb, t_bor, D)):
            for blk in range(width // 512):
                tm = prog.op("tensor", lambda e, b=blk, o=pb, r=row_sb:
                             e.matmul(o[:], ones1[:], r[:, b * 512:(b + 1) * 512],
                                      start=True, stop=True),
                             waits=later(t_o1, row_tok, tprev))
                tprev = prog.op("vector", lambda e, b=blk, o=dst, i=pb:
                                e.tensor_copy(o[:, b * 512:(b + 1) * 512], i[:]),
                                waits=later(tm))
        bo_tok = tprev
        state["pb_busy"] = tprev
    else:
        bq_sb = bk_sb = bv_bc = bo_bc = bo_tok = None

    def proj_fm(w_src, nm, bias_sb, scr, row_sb, row_tok, xhat_ready, t0):
        st = state
        toks = row_tok
        for mb in range((nm + 3) // 4):
            ms = list(range(mb * 4, min(mb * 4 + 4, nm)))
            last_mm = [None] * len(ms)
            for c in range(NCHUNK):
                bw = c % 2
                tw = prog.op("gpsimd",
                             lambda e, d=wtile[bw], s=w_src, c=c, mb=mb,
                             w=len(ms) * P:
                             e.dma_start(out=d[:, 0:w],
                                         in_=s[c * P:(c + 1) * P,
                                               mb * 512:mb * 512 + w]),
                             waits=later(st["w_busy"][bw]), dma=True, dsem=f"wt{bw}")
                for jj in range(len(ms)):
                    wl = ()
                    if jj == 0:
                        wl = later(tw, xhat_ready,
                                   *(st["bank_busy"] if c == 0 else []))
                    last_mm[jj] = prog.op("tensor",
                                          lambda e, w=wtile[bw], jj=jj, c=c,
                                          o=banks[jj]:
                                          e.matmul(o[:], w[:, jj * P:(jj + 1) * P],
                                                   xhat_fm[:, c * 512:(c + 1) * 512],
                                                   start=(c == 0),
                                                   stop=(c == NCHUNK - 1)),
                                          waits=wl)
                st["w_busy"][bw] = last_mm[len(ms) - 1]
            for jj, m in enumerate(ms):
                qk = m % 2
                if bias_sb is not None:
                    tev = prog.op("vector", lambda e, o=qsb[qk], i=banks[jj],
                                  b=bias_sb, m=m:
                                  e.tensor_scalar(o[:], i[:], b[:, m:m + 1], None,
                                                  ALU.add),
                                  waits=later(last_mm[jj], *st["q_busy"][qk]))
                else:
                    tev = prog.op("scalar", lambda e, o=qsb[qk], i=banks[jj]:
                                  e.activation(o[:], i[:], AF.Copy),
                                  waits=later(last_mm[jj], *st["q_busy"][qk]))
                st["bank_busy"][jj] = tev
                st_toks = []
                for j in range(hpc):
                    h = m * hpc + j
                    st_toks.append(prog.op("sync",
                                   lambda e, src=qsb[qk], j=j, h=h, scr=scr, t0=t0:
                                   e.dma_start(out=scr[h][:, t0:t0 + 512],
                                               in_=src[j * dh:(j + 1) * dh, :]),
                                   waits=later(tev), dma=True, dsem=f"qs{qk}"))
                tsq = prog.op("scalar", lambda e, o=q2sb, i=qsb[qk]:
                              e.activation(o[:], i[:], AF.Square),
                              waits=later(tev, st["q2_busy"]))
                st["q_busy"][qk] = [tsq, st_toks[-1]]
                for j in range(hpc):
                    h = m * hpc + j
                    tn = prog.op("tensor", lambda e, j=j, o=pnorm:
                                 e.matmul(o[:], sel_sb[:, j:j + 1], q2sb[:],
                                          start=True, stop=True),
                                 waits=later(tsq, t_sel, st["pnorm_busy"]))
                    st["q2_busy"] = tn
                    tr = prog.op("vector", lambda e, o=nrow, i=pnorm:
                                 e.reduce_max(o[:], i[:], axis=AX.X),
                                 waits=later(tn, st["nrow_busy"]))
                    st["pnorm_busy"] = tr
                    tmx = prog.op("vector", lambda e, o=row_sb, h=h, i=nrow:
                                  e.tensor_tensor(o[0:1, h:h + 1], o[0:1, h:h + 1],
                                                  i[:], ALU.max),
                                  waits=later(tr, toks))
                    st["nrow_busy"] = tmx
                    toks = tmx
        return toks

    x_busy4 = [None] * 4
    xhat_busy = None
    for g in range(ngroups):
        t_fm = None
        for t in range(4):
            guards = [x_busy4[t]]
            if t == 0:
                guards.append(xhat_busy)
            lnb["xsem"] = f"xk{t}"
            _, t_fm = emit_ln_xhat(ctx, x_d, g * 4 + t, lnb, xhat_fm, t * P,
                                   xbufs[t], x_write_guards=guards)
            x_busy4[t] = t_fm
        xhat_ready = t_fm

        mq_tok = proj_fm(wq_d, NMQ, bq_sb, q_scr, mqrow, mq_tok, xhat_ready,
                         g * 512)
        mk_tok = proj_fm(wk_d, NMK, bk_sb, k_scr, mkrow, mk_tok, xhat_ready,
                         g * 512)

        for vb in range(GD // 512 if GD >= 512 else 1):
            w = min(512, GD)
            last_mm = None
            for c in range(NCHUNK):
                bw = c % 2
                tw = prog.op("gpsimd",
                             lambda e, d=wtile[bw], s=wv_d, c=c, vb=vb, w=w:
                             e.dma_start(out=d[:, 0:w],
                                         in_=s[c * P:(c + 1) * P,
                                               vb * 512:vb * 512 + w]),
                             waits=later(state["w_busy"][bw]), dma=True,
                             dsem=f"wt{bw}")
                for t in range(4):
                    wl = ()
                    if t == 0:
                        wl = later(tw, xhat_ready,
                                   *(state["bank_busy"] if c == 0 else []))
                    last_mm = prog.op("tensor",
                                      lambda e, wti=wtile[bw], t=t, c=c,
                                      o=banks[t], w=w:
                                      e.matmul(o[:, 0:w],
                                               xhat_fm[:, c * 512 + t * P:
                                                       c * 512 + (t + 1) * P],
                                               wti[:, 0:w], start=(c == 0),
                                               stop=(c == NCHUNK - 1)),
                                      waits=wl)
                state["w_busy"][bw] = last_mm
            for t in range(4):
                qk = t % 2
                tile = g * 4 + t
                if has_bias:
                    tev = prog.op("vector",
                                  lambda e, o=qsb[qk], i=banks[t], vb=vb, w=w:
                                  e.tensor_tensor(o[:, 0:w], i[:, 0:w],
                                                  bv_bc[:, vb * 512:vb * 512 + w],
                                                  ALU.add),
                                  waits=later(last_mm, state["pb_busy"],
                                              *state["q_busy"][qk]))
                else:
                    tev = prog.op("vector",
                                  lambda e, o=qsb[qk], i=banks[t], w=w:
                                  e.tensor_copy(o[:, 0:w], i[:, 0:w]),
                                  waits=later(last_mm, *state["q_busy"][qk]))
                state["bank_busy"][t] = tev
                st_toks = []
                for kv in range(w // dh):
                    kvh = vb * (512 // dh) + kv
                    st_toks.append(prog.op("sync",
                                   lambda e, src=qsb[qk], kv=kv, kvh=kvh, tile=tile:
                                   e.dma_start(out=v_scr[kvh][:, tile * dh:
                                                              (tile + 1) * dh],
                                               in_=src[:, kv * dh:(kv + 1) * dh]),
                                   waits=later(tev), dma=True, dsem=f"qs{qk}"))
                state["q_busy"][qk] = [st_toks[-1]]
        xhat_busy = last_mm

    prog.barrier()

    ksb = [ctx.sb("ksb", [dh, T], BF16) for _ in range(2)]
    vsb = [ctx.sb("vsb", [P, ktiles * dh], BF16) for _ in range(2)]
    qblk = [ctx.sb("qblk", [dh, 512], BF16) for _ in range(2)]
    ptb = [ctx.sb("ptb", [P, 512], BF16) for _ in range(2)]
    zr = ctx.sb("zr", [1, 512], F32)
    bcz = ctx.sb("bcz", [P, 512], F32)
    asb = [ctx.sb("asb", [P, 512], BF16) for _ in range(2)]
    bias_col = ctx.sb("bias_col", [P, 1], F32)
    mh = ctx.sb("mh", [1, 1], F32)
    lnc, t_lnc = ctx.const_col(float(np.log(1.01 * s)))

    st_banks = banks[0:2]
    pv_bank = banks[2]
    z_bank = pnorm
    bc_bank = banks[3]

    kv_busy = [None, None]
    q_blk_busy = [None, None]
    pt_busy = [None, None]
    st_bank_busy = [None, None]
    pv_busy = None
    z_busy = None
    bc_busy = None
    asb_busy = [None, None]
    bias_busy = None
    mh_busy = None
    zr_busy = None
    bcz_busy = None

    for kvh in range(G):
        kb = kvh % 2
        tk = prog.op("sync", lambda e, d=ksb[kb], s=k_scr, kvh=kvh:
                     e.dma_start(out=d[:], in_=s[kvh]),
                     waits=later(kv_busy[kb]), dma=True, dsem=f"kv{kb}")
        tv = prog.op("sync", lambda e, d=vsb[kb], s=v_scr, kvh=kvh:
                     e.dma_start(out=d[:], in_=s[kvh]),
                     waits=later(kv_busy[kb]), dma=True, dsem=f"vv{kb}")
        last_kv_reader = None
        for hq in range(H // G):
            h = hq * G + kvh

            t1 = prog.op("vector", lambda e, o=mh, h=h, kvh=kvh:
                         e.tensor_tensor(o[:], mqrow[0:1, h:h + 1],
                                         mkrow[0:1, kvh:kvh + 1], ALU.mult),
                         waits=later(mq_tok, mk_tok, mh_busy))
            t2 = prog.op("scalar", lambda e, o=mh: e.activation(o[:], o[:], AF.Ln),
                         waits=later(t1))
            t3 = prog.op("scalar", lambda e, o=mh, b=lnc:
                         e.activation(o[:], o[:], AF.Exp, bias=b[0:1, :], scale=0.5),
                         waits=later(t2, t_lnc))
            t4 = prog.op("tensor", lambda e, o=bc_bank:
                         e.matmul(o[:, 0:1], negones1[:], mh[:], start=True,
                                  stop=True),
                         waits=later(t3, t_n1, bc_busy))
            mh_busy = t4
            t5 = prog.op("vector", lambda e, o=bias_col, i=bc_bank:
                         e.tensor_copy(o[:], i[:, 0:1]),
                         waits=later(t4, bias_busy))
            bc_busy = t5
            tbias = t5
            for qb in range(T // 512):
                qkb = qb % 2
                tq = prog.op("sync", lambda e, d=qblk[qkb], s=q_scr, h=h, qb=qb:
                             e.dma_start(out=d[:],
                                         in_=s[h][:, qb * 512:(qb + 1) * 512]),
                             waits=later(q_blk_busy[qkb]), dma=True,
                             dsem=f"qb{qkb}")
                ktmax = 4 * qb + 4
                t_pv = None
                t_z = None
                for kt in range(ktmax):
                    sb_ = kt % 2
                    tmm = prog.op("tensor", lambda e, o=st_banks[sb_], kt=kt,
                                  k=ksb[kb], q=qblk[qkb]:
                                  e.matmul(o[:], k[:, kt * P:(kt + 1) * P], q[:],
                                           start=True, stop=True),
                                  waits=later(tk, tq, st_bank_busy[sb_]))
                    texp = prog.op("scalar", lambda e, o=ptb[sb_], i=st_banks[sb_],
                                   b=bias_col:
                                   e.activation(o[:], i[:], AF.Exp, bias=b[:],
                                                scale=s),
                                   waits=later(tmm, tbias, pt_busy[sb_]))
                    bias_busy = texp
                    st_bank_busy[sb_] = texp
                    tp = texp
                    if kt >= 4 * qb:
                        r = kt - 4 * qb
                        tp = prog.op("vector", lambda e, o=ptb[sb_], r=r:
                                     e.tensor_tensor(o[:], o[:],
                                                     dm_sb[:, r * 512:(r + 1) * 512],
                                                     ALU.mult),
                                     waits=later(texp, t_dm))
                    t_pv = prog.op("tensor", lambda e, o=pv_bank, kt=kt, v=vsb[kb],
                                   pt=ptb[sb_], first=(kt == 0),
                                   last=(kt == ktmax - 1):
                                   e.matmul(o[0:dh, :], v[:, kt * dh:(kt + 1) * dh],
                                            pt[:], start=first, stop=last),
                                   waits=later(tp, tv,
                                               pv_busy if kt == 0 else None))
                    t_z = prog.op("tensor", lambda e, o=z_bank, kt=kt, pt=ptb[sb_],
                                  first=(kt == 0), last=(kt == ktmax - 1):
                                  e.matmul(o[:], onescol_bf[:], pt[:], start=first,
                                           stop=last),
                                  waits=later(t_oc, z_busy if kt == 0 else None))
                    pt_busy[sb_] = t_z
                last_kv_reader = t_z
                q_blk_busy[qkb] = t_z
                trc = prog.op("vector", lambda e, o=zr, i=z_bank:
                              e.reciprocal(o[:], i[:]), waits=later(t_z, zr_busy))
                z_busy = trc
                tbc = prog.op("tensor", lambda e, o=bc_bank:
                              e.matmul(o[:], ones1[:], zr[:], start=True, stop=True),
                              waits=later(trc, t_o1, bc_busy))
                zr_busy = tbc
                tcp = prog.op("vector", lambda e, o=bcz, i=bc_bank:
                              e.tensor_copy(o[:], i[:]), waits=later(tbc, bcz_busy))
                bc_busy = tcp
                ab = qb % 2
                tev = prog.op("vector", lambda e, o=asb[ab], i=pv_bank, b=bcz:
                              e.tensor_tensor(o[0:dh, :], i[0:dh, :], b[0:dh, :],
                                              ALU.mult),
                              waits=later(tcp, t_pv, asb_busy[ab]))
                bcz_busy = tev
                pv_busy = tev
                st_toks = []
                poff = (h * dh) % P
                cc = (h * dh) // P
                for j in range(4):
                    tile = qb * 4 + j
                    st_toks.append(prog.op("sync",
                                   lambda e, src=asb[ab], j=j, tile=tile,
                                   poff=poff, cc=cc:
                                   e.dma_start(
                                       out=a_scr[tile].rearrange(
                                           "p (c w) -> p c w", w=P)[
                                           poff:poff + dh, cc, :],
                                       in_=src[0:dh, j * P:(j + 1) * P]),
                                   waits=later(tev), dma=True, dsem=f"as{ab}"))
                asb_busy[ab] = st_toks[-1]
        kv_busy[kb] = last_kv_reader

    prog.barrier()

    emit_out_proj(ctx, a_scr, wo_d, x_d, y_d, T, NC, banks, bo_bc=bo_bc,
                  bo_tok=bo_tok, x_sb=xbufs)

    for s_, v in list(prog.cnt.items()):
        prog.wait("sync", s_, v)
    prog.build()
    return nc


def build_gla(T):
    NH = 4
    nchunks = T // P
    ngroups = T // 512
    lnS = float(np.log(1.0 / 16.0))

    nc = bass.Bass("TRN2", target_bir_lowering=False)
    x_d = nc.dram_tensor("x", [T, D], F32, kind="ExternalInput").ap()
    wq_d = nc.dram_tensor("wq", [D, 1024], F32, kind="ExternalInput").ap()
    wk_d = nc.dram_tensor("wk", [D, 1024], F32, kind="ExternalInput").ap()
    wv_d = nc.dram_tensor("wv", [D, 2048], F32, kind="ExternalInput").ap()
    wg_d = nc.dram_tensor("wg", [D, 2048], F32, kind="ExternalInput").ap()
    wo_d = nc.dram_tensor("wo", [D, D], F32, kind="ExternalInput").ap()
    wgk1_d = nc.dram_tensor("wgk1t", [P, NCHUNK * 16], F32, kind="ExternalInput").ap()
    wgk2_d = nc.dram_tensor("wgk2", [16, 1024], F32, kind="ExternalInput").ap()
    bgk2_d = nc.dram_tensor("bgk2_row", [1, 1024], F32, kind="ExternalInput").ap()
    gam_d = nc.dram_tensor("gamma2", [P, 4], F32, kind="ExternalInput").ap()
    id_d = nc.dram_tensor("ident", [P, P], F32, kind="ExternalInput").ap()
    ut_d = nc.dram_tensor("ut", [P, P], F32, kind="ExternalInput").ap()
    y_d = nc.dram_tensor("y", [T, D], F32, kind="ExternalOutput").ap()

    q_scr = nc.dram_tensor("q_scr", [T, 1024], F32, kind="Internal").ap()
    k_scr = nc.dram_tensor("k_scr", [T, 1024], F32, kind="Internal").ap()
    v_scr = nc.dram_tensor("v_scr", [T, 2048], F32, kind="Internal").ap()
    gk_scr = nc.dram_tensor("gk_scr", [T, 1024], F32, kind="Internal").ap()
    gs_scr = nc.dram_tensor("gs_scr", [D, T], BF16, kind="Internal").ap()
    og_scr = nc.dram_tensor("og_scr", [T // P, P, NCHUNK * P], BF16,
                            kind="Internal").ap()

    ctx = Ctx(nc)
    prog = ctx.prog

    identb, t_idb = emit_const_load(ctx, id_d, [P, P], BF16, "identb")
    identf, t_idf = emit_const_load(ctx, id_d, [P, P], F32, "identf")
    ut_sb, t_ut = emit_const_load(ctx, ut_d, [P, P], F32, "ut")
    gam_sb, t_gam = emit_const_load(ctx, gam_d, [P, 4], F32, "gam")
    wgk1_sb, t_w1 = emit_const_load(ctx, wgk1_d, [P, NCHUNK * 16], BF16, "wgk1")
    wgk2_sb, t_w2 = emit_const_load(ctx, wgk2_d, [16, 1024], BF16, "wgk2")
    bgk2_row, t_bg = emit_const_load(ctx, bgk2_d, [1, 1024], F32, "bgk2r")

    lnb = make_ln_bufs(ctx, 512)
    lnb["ident"] = identb
    lnb["prev_xhat_reader"] = t_idb

    ones1 = ctx.sb("ones1", [1, P], F32)
    t_o1 = prog.op("vector", lambda e, t=ones1: e.memset(t[:], 1.0))
    onescol = ctx.sb("onescol", [P, 1], F32)
    t_oc = prog.op("vector", lambda e, t=onescol: e.memset(t[:], 1.0))
    one_col, t_one = ctx.const_col(1.0)
    eps_col, t_eps = ctx.const_col(LN_EPS)
    lnS_col, t_lnS = ctx.const_col(lnS)

    xhat_fm = ctx.sb("xhat_fm", [P, NCHUNK * 512], BF16)
    xbufs = [ctx.sb("xk", [P, D], F32) for _ in range(4)]
    wtile = [ctx.sb("wtile", [P, 512], BF16) for _ in range(2)]
    evb = [ctx.sb("evb", [P, 512], F32) for _ in range(2)]
    evbh = [ctx.sb("evbh", [P, 512], BF16) for _ in range(2)]
    u_sb = ctx.sb("u_sb", [16, 512], BF16)
    zt_sb = ctx.sb("zt", [P, 512], F32)
    t3_sb = ctx.sb("t3", [P, 512], F32)
    bgk_bc = ctx.sb("bgk_bc", [P, 1024], F32)

    FB = [ctx.ps("fb", [P, 512]) for _ in range(6)]
    pfree = [None] * 6

    def pwrite(idx, tok):
        pfree[idx] = tok

    st = {"w_busy": [None, None], "ev_busy": [[], []], "evh_busy": [[], []],
          "bank_busy": [None] * 4, "u_busy": None, "z_busy": None,
          "t3_busy": None, "zt_r": None}

    tprev = None
    for blk in range(2):
        tm = prog.op("tensor", lambda e, b=blk, o=FB[5]:
                     e.matmul(o[:], ones1[:], bgk2_row[0:1, b * 512:(b + 1) * 512],
                              start=True, stop=True),
                     waits=later(t_o1, t_bg, tprev))
        tprev = prog.op("vector", lambda e, b=blk, o=bgk_bc, i=FB[5]:
                        e.tensor_copy(o[:, b * 512:(b + 1) * 512], i[:]),
                        waits=later(tm))
    pwrite(5, tprev)

    def proj_tm(w_src, wout, scr, xhat_ready, t0, sem_pfx):
        for nb in range(wout // 512):
            last_mm = None
            for c in range(NCHUNK):
                bw = c % 2
                tw = prog.op("gpsimd",
                             lambda e, d=wtile[bw], s=w_src, c=c, nb=nb:
                             e.dma_start(out=d[:], in_=s[c * P:(c + 1) * P,
                                                         nb * 512:(nb + 1) * 512]),
                             waits=later(st["w_busy"][bw]), dma=True, dsem=f"wt{bw}")
                for t in range(4):
                    wl = ()
                    if t == 0:
                        wl = later(tw, xhat_ready,
                                   *(st["bank_busy"] if c == 0 else []))
                    last_mm = prog.op("tensor",
                                      lambda e, w=wtile[bw], t=t, c=c, o=FB[t]:
                                      e.matmul(o[:],
                                               xhat_fm[:, c * 512 + t * P:
                                                       c * 512 + (t + 1) * P],
                                               w[:], start=(c == 0),
                                               stop=(c == NCHUNK - 1)),
                                      waits=wl)
                st["w_busy"][bw] = last_mm
            for t in range(4):
                eb = t % 2
                tev = prog.op("vector", lambda e, o=evb[eb], i=FB[t]:
                              e.tensor_copy(o[:], i[:]),
                              waits=later(last_mm, *st["ev_busy"][eb]))
                st["bank_busy"][t] = tev
                tst = prog.op("sync", lambda e, src=evb[eb], t0=t0, t=t, nb=nb,
                              scr=scr:
                              e.dma_start(out=scr[t0 + t * P:t0 + (t + 1) * P,
                                                  nb * 512:(nb + 1) * 512],
                                          in_=src[:]),
                              waits=later(tev), dma=True, dsem=f"ev{eb}")
                st["ev_busy"][eb] = [tst]

    def proj_fm_silu(w_src, scr, xhat_ready, t0):
        for mb in range(4):
            last_mm = [None] * 4
            for c in range(NCHUNK):
                bw = c % 2
                tw = prog.op("gpsimd",
                             lambda e, d=wtile[bw], s=w_src, c=c, mb=mb:
                             e.dma_start(out=d[:], in_=s[c * P:(c + 1) * P,
                                                         mb * 512:(mb + 1) * 512]),
                             waits=later(st["w_busy"][bw]), dma=True, dsem=f"wt{bw}")
                for jj in range(4):
                    wl = ()
                    if jj == 0:
                        wl = later(tw, xhat_ready,
                                   *(st["bank_busy"] if c == 0 else []))
                    last_mm[jj] = prog.op("tensor",
                                          lambda e, w=wtile[bw], jj=jj, c=c,
                                          o=FB[jj]:
                                          e.matmul(o[:], w[:, jj * P:(jj + 1) * P],
                                                   xhat_fm[:, c * 512:(c + 1) * 512],
                                                   start=(c == 0),
                                                   stop=(c == NCHUNK - 1)),
                                          waits=wl)
                st["w_busy"][bw] = last_mm[3]
            for jj in range(4):
                m = mb * 4 + jj
                eb = jj % 2
                tev = prog.op("scalar", lambda e, o=evbh[eb], i=FB[jj]:
                              e.activation(o[:], i[:], AF.Silu),
                              waits=later(last_mm[jj], *st["evh_busy"][eb]))
                st["bank_busy"][jj] = tev
                tst = prog.op("sync", lambda e, src=evbh[eb], m=m, t0=t0, scr=scr:
                              e.dma_start(out=scr[m * P:(m + 1) * P, t0:t0 + 512],
                                          in_=src[:]),
                              waits=later(tev), dma=True, dsem=f"eh{eb}")
                st["evh_busy"][eb] = [tst]

    x_busy4 = [None] * 4
    xhat_busy = None
    for g in range(ngroups):
        t_fm = None
        for t in range(4):
            guards = [x_busy4[t]]
            if t == 0:
                guards.append(xhat_busy)
            lnb["xsem"] = f"xk{t}"
            _, t_fm = emit_ln_xhat(ctx, x_d, g * 4 + t, lnb, xhat_fm, t * P,
                                   xbufs[t], x_write_guards=guards)
            x_busy4[t] = t_fm
        xhat_ready = t_fm
        t0 = g * 512

        proj_tm(wq_d, 1024, q_scr, xhat_ready, t0, "q")
        proj_tm(wk_d, 1024, k_scr, xhat_ready, t0, "k")
        proj_tm(wv_d, 2048, v_scr, xhat_ready, t0, "v")

        tu = None
        for c in range(NCHUNK):
            wl = later(t_w1, xhat_ready, pfree[5]) if c == 0 else ()
            tu = prog.op("tensor", lambda e, c=c, o=FB[5]:
                         e.matmul(o[0:16, :], wgk1_sb[:, c * 16:(c + 1) * 16],
                                  xhat_fm[:, c * 512:(c + 1) * 512],
                                  start=(c == 0), stop=(c == NCHUNK - 1)),
                         waits=wl)
        tue = prog.op("vector", lambda e, o=u_sb, i=FB[5]:
                      e.tensor_copy(o[:], i[0:16, :]),
                      waits=later(tu, st["u_busy"]))
        pwrite(5, tue)
        for t in range(4):
            for nb in range(2):
                bk = 4 if (t + nb) % 2 == 0 else 5
                tz = prog.op("tensor", lambda e, t=t, nb=nb, o=FB[bk]:
                             e.matmul(o[:], u_sb[:, t * P:(t + 1) * P],
                                      wgk2_sb[:, nb * 512:(nb + 1) * 512],
                                      start=True, stop=True),
                             waits=later(tue, t_w2, pfree[bk]))
                ta = prog.op("vector", lambda e, o=zt_sb, i=FB[bk], nb=nb:
                             e.tensor_tensor(o[:], i[:],
                                             bgk_bc[:, nb * 512:(nb + 1) * 512],
                                             ALU.add),
                             waits=later(tz, tprev, st["z_busy"]))
                pwrite(bk, ta)
                tb = prog.op("scalar", lambda e, o=t3_sb, i=zt_sb:
                             e.activation(o[:], i[:], AF.Abs),
                             waits=later(ta, st["t3_busy"]))
                tc = prog.op("scalar", lambda e, o=t3_sb:
                             e.activation(o[:], o[:], AF.Exp, scale=-1.0),
                             waits=later(tb))
                td = prog.op("scalar", lambda e, o=t3_sb, b=one_col:
                             e.activation(o[:], o[:], AF.Ln, bias=b[:]),
                             waits=later(tc, t_one))
                te = prog.op("vector", lambda e, o=zt_sb, i=zt_sb:
                             e.tensor_scalar(o[:], i[:], 0.0, 1.0 / 16.0,
                                             ALU.min, ALU.mult),
                             waits=later(td))
                tf = prog.op("vector", lambda e, o=zt_sb, a=t3_sb:
                             e.scalar_tensor_tensor(o[:], a[:], -1.0 / 16.0, o[:],
                                                    ALU.mult, ALU.add),
                             waits=later(te))
                st["t3_busy"] = tf
                tst = prog.op("sync", lambda e, src=zt_sb, t=t, nb=nb, t0=t0:
                              e.dma_start(out=gk_scr[t0 + t * P:t0 + (t + 1) * P,
                                                     nb * 512:(nb + 1) * 512],
                                          in_=src[:]),
                              waits=later(tf), dma=True, dsem="zs")
                st["z_busy"] = tst

        u_readers = tz
        st["u_busy"] = u_readers
        proj_fm_silu(wg_d, gs_scr, xhat_ready, t0)
        xh0, xh1 = st["w_busy"]
        xhat_busy = xh1 if (xh0 is None or (xh1 and xh1[1] > xh0[1])) else xh0

    prog.barrier()

    qld = [ctx.sb("qld", [P, 1024], F32) for _ in range(2)]
    kld = [ctx.sb("kld", [P, 1024], F32) for _ in range(2)]
    gld = [ctx.sb("gld", [P, 1024], F32) for _ in range(2)]
    vld = [ctx.sb("vld", [P, 2048], F32) for _ in range(2)]
    gst = [ctx.sb("gst", [P, NCHUNK * P], BF16) for _ in range(2)]
    gc_sb = ctx.sb("gc_sb", [P, 1024], F32)
    e_sb = ctx.sb("e_sb", [P, 1024], F32)
    qg_sb = ctx.sb("qg_sb", [P, 1024], F32)
    kg_sb = ctx.sb("kg_sb", [P, 1024], F32)
    kd_sb = ctx.sb("kd_sb", [P, 1024], F32)
    qgf_sb = ctx.sb("qgf_sb", [P, 1024], F32)
    kgf_sb = ctx.sb("kgf_sb", [P, 1024], F32)
    glr_sb = ctx.sb("glr_sb", [1, 1024], F32)
    am_sb = ctx.sb("am_sb", [P, P], F32)
    o2_sb = ctx.sb("o2_sb", [P, 512], F32)
    rr_sb = ctx.sb("rr_sb", [1, P], F32)
    bcr_sb = ctx.sb("bcr_sb", [P, P], F32)
    ec_sb = ctx.sb("ec_sb", [P, 2], F32)
    t1g_sb = ctx.sb("t1g", [P, P], F32)
    og_sb = [ctx.sb("og_sb", [P, P], BF16) for _ in range(2)]
    S_all = ctx.sb("S_all", [P, NH * 2 * 512], F32)

    t_s0 = None
    for h in range(NH * 2):
        t_s0 = prog.op("vector", lambda e, h=h, o=S_all:
                       e.memset(o[:, h * 512:(h + 1) * 512], 0.0))

    busy = {"gc": None, "e": None, "qg": None, "kg": None, "kd": None,
            "qgf": None, "kgf": None, "glr": None, "am": None, "o2": None,
            "rr": None, "bcr": None, "ec": None, "t1g": None,
            "og": [None, None],
            "q": [None, None], "k": [None, None], "g": [None, None],
            "v": [None, None], "gs": [None, None], "S": t_s0}
    pfree = [None] * 6

    for chk in range(nchunks):
        lb = chk % 2
        r0 = chk * P
        tql = prog.op("sync", lambda e, d=qld[lb], r0=r0:
                      e.dma_start(out=d[:], in_=q_scr[r0:r0 + P, :]),
                      waits=later(busy["q"][lb]), dma=True, dsem=f"ql{lb}")
        tkl = prog.op("sync", lambda e, d=kld[lb], r0=r0:
                      e.dma_start(out=d[:], in_=k_scr[r0:r0 + P, :]),
                      waits=later(busy["k"][lb]), dma=True, dsem=f"kl{lb}")
        tgl = prog.op("sync", lambda e, d=gld[lb], r0=r0:
                      e.dma_start(out=d[:], in_=gk_scr[r0:r0 + P, :]),
                      waits=later(busy["g"][lb]), dma=True, dsem=f"gl{lb}")
        tvl = prog.op("sync", lambda e, d=vld[lb], r0=r0:
                      e.dma_start(out=d[:], in_=v_scr[r0:r0 + P, :]),
                      waits=later(busy["v"][lb]), dma=True, dsem=f"vl{lb}")
        tgs = None
        for c in range(NCHUNK):
            tgs = prog.op("sync", lambda e, d=gst[lb], c=c, r0=r0:
                          e.dma_start(out=d[:, c * P:(c + 1) * P],
                                      in_=gs_scr[c * P:(c + 1) * P, r0:r0 + P]),
                          waits=later(busy["gs"][lb]) if c == 0 else (),
                          dma=True, dsem=f"gsl{lb}")

        tgc = [None, None]
        for half in range(2):
            tgc[half] = prog.op("tensor", lambda e, h=half, o=FB[half], g_=gld[lb]:
                                e.matmul(o[:], ut_sb[:],
                                         g_[:, h * 512:(h + 1) * 512],
                                         start=True, stop=True),
                                waits=later(tgl, t_ut, pfree[half]))
        tgce = prog.op("vector", lambda e, o=gc_sb, i0=FB[0], i1=FB[1]:
                       [e.tensor_copy(o[:, 0:512], i0[:]),
                        e.tensor_copy(o[:, 512:1024], i1[:])][-1],
                       waits=later(tgc[0], tgc[1], busy["gc"]))
        pwrite(0, tgce); pwrite(1, tgce)
        tglr = [None, None]
        for half in range(2):
            tglr[half] = prog.op("tensor",
                                 lambda e, h=half, o=FB[2 + half], g_=gld[lb]:
                                 e.matmul(o[0:1, :], onescol[:],
                                          g_[:, h * 512:(h + 1) * 512],
                                          start=True, stop=True),
                                 waits=later(tgl, t_oc, pfree[2 + half]))
        tglre = prog.op("vector", lambda e, o=glr_sb, i0=FB[2], i1=FB[3]:
                        [e.tensor_copy(o[0:1, 0:512], i0[0:1, :]),
                         e.tensor_copy(o[0:1, 512:1024], i1[0:1, :])][-1],
                        waits=later(tglr[0], tglr[1], busy["glr"]))
        pwrite(2, tglre); pwrite(3, tglre)
        tglx = prog.op("scalar", lambda e, o=glr_sb:
                       e.activation(o[:], o[:], AF.Exp), waits=later(tglre))

        teq = prog.op("scalar", lambda e, o=e_sb, i=gc_sb, b=lnS_col:
                      e.activation(o[:], i[:], AF.Exp, bias=b[:]),
                      waits=later(tgce, t_lnS, busy["e"]))
        tqg = prog.op("vector", lambda e, o=qg_sb, a=qld[lb], b=e_sb:
                      e.tensor_tensor(o[:], a[:], b[:], ALU.mult),
                      waits=later(teq, tql, busy["qg"]))
        busy["q"][lb] = tqg
        tek = prog.op("scalar", lambda e, o=e_sb, i=gc_sb:
                      e.activation(o[:], i[:], AF.Exp, scale=-1.0),
                      waits=later(tqg, tgce))
        busy["gc"] = tek
        tkg = prog.op("vector", lambda e, o=kg_sb, a=kld[lb], b=e_sb:
                      e.tensor_tensor(o[:], a[:], b[:], ALU.mult),
                      waits=later(tek, tkl, busy["kg"]))
        busy["k"][lb] = tkg
        busy["e"] = tkg
        tbc = [None, None]
        for half in range(2):
            tbc[half] = prog.op("tensor", lambda e, h=half, o=FB[2 + half]:
                                e.matmul(o[:], ones1[:],
                                         glr_sb[0:1, h * 512:(h + 1) * 512],
                                         start=True, stop=True),
                                waits=later(tglx, t_o1, pfree[2 + half]))
        tkd = prog.op("vector", lambda e, o=kd_sb, a=kg_sb, b0=FB[2], b1=FB[3]:
                      [e.tensor_tensor(o[:, 0:512], a[:, 0:512], b0[:], ALU.mult),
                       e.tensor_tensor(o[:, 512:1024], a[:, 512:1024], b1[:],
                                       ALU.mult)][-1],
                      waits=later(tkg, tbc[0], tbc[1], busy["kd"]))
        pwrite(2, tkd); pwrite(3, tkd)

        tqgf = tkgf = None
        for (src_sb, dst_sb, key, src_tok, bset) in (
                (qg_sb, qgf_sb, "qgf", tqg, (0, 1)),
                (kg_sb, kgf_sb, "kgf", tkg, (2, 3))):
            for rnd in range(2):
                bnk = bset[rnd]
                tmm = None
                for j in range(4):
                    cidx = rnd * 4 + j
                    wl = later(src_tok, t_idf, pfree[bnk],
                               tkd if bset[0] == 2 else None) if j == 0 else ()
                    tmm = prog.op("tensor",
                                  lambda e, b=FB[bnk], j=j, c=cidx, s_=src_sb:
                                  e.transpose(b[:, j * P:(j + 1) * P],
                                              s_[:, c * P:(c + 1) * P],
                                              identf[:]),
                                  waits=wl)
                tcp = prog.op("vector", lambda e, b=FB[bnk], d=dst_sb, rnd=rnd:
                              e.tensor_copy(d[:, rnd * 512:(rnd + 1) * 512], b[:]),
                              waits=later(tmm, busy[key] if rnd == 0 else None))
                pwrite(bnk, tcp)
                if key == "qgf":
                    tqgf = tcp
                else:
                    tkgf = tcp
        busy["qg"] = tqgf
        busy["kg"] = later(tkgf, tkd)[0]
        tr_qg_last = tqgf

        last_S = busy["S"]
        last_o_mm = None
        last_contrib = None
        last_gate = None
        for h in range(NH):
            tA = None
            for dkc in range(2):
                wl = later(tqgf, tkgf, pfree[0]) if dkc == 0 else ()
                tA = prog.op("tensor", lambda e, dkc=dkc, h=h, o=FB[0]:
                             e.matmul(o[:, 0:P],
                                      kgf_sb[:, (2 * h + dkc) * P:
                                             (2 * h + dkc + 1) * P],
                                      qgf_sb[:, (2 * h + dkc) * P:
                                             (2 * h + dkc + 1) * P],
                                      start=(dkc == 0), stop=(dkc == 1)),
                             waits=wl)
            tAm = prog.op("vector", lambda e, o=am_sb, i=FB[0]:
                          e.tensor_tensor(o[:], i[:, 0:P], ut_sb[:], ALU.mult),
                          waits=later(tA, t_ut, busy["am"]))
            pwrite(0, tAm)
            tO = None
            for s_ in range(4):
                for dkc in range(2):
                    wl = ()
                    if s_ == 0 and dkc == 0:
                        wl = later(tqgf, last_S, pfree[4])
                    tO = prog.op("tensor",
                                 lambda e, s_=s_, dkc=dkc, h=h, o=FB[4]:
                                 e.matmul(o[:, s_ * P:(s_ + 1) * P],
                                          S_all[:, (h * 2 + dkc) * 512 + s_ * P:
                                                (h * 2 + dkc) * 512 + (s_ + 1) * P],
                                          qgf_sb[:, (2 * h + dkc) * P:
                                                 (2 * h + dkc + 1) * P],
                                          start=(dkc == 0), stop=False),
                                 waits=wl)
                tO = prog.op("tensor", lambda e, s_=s_, h=h, o=FB[4], v=vld[lb]:
                             e.matmul(o[:, s_ * P:(s_ + 1) * P],
                                      v[:, h * 512 + s_ * P:h * 512 + (s_ + 1) * P],
                                      am_sb[:], start=False, stop=True),
                             waits=later(tAm, tvl))
            last_o_mm = tO
            busy["am"] = tO
            for dkc in range(2):
                tc_ = prog.op("tensor", lambda e, dkc=dkc, h=h, o=FB[2 + dkc],
                              v=vld[lb]:
                              e.matmul(o[:], kd_sb[:, (2 * h + dkc) * P:
                                                  (2 * h + dkc + 1) * P],
                                       v[:, h * 512:(h + 1) * 512],
                                       start=True, stop=True),
                              waits=later(tkd, tvl, pfree[2 + dkc]))
                last_contrib = tc_
                tec = prog.op("tensor", lambda e, dkc=dkc, h=h, o=FB[1]:
                              e.matmul(o[:, dkc:dkc + 1],
                                       glr_sb[0:1, (2 * h + dkc) * P:
                                              (2 * h + dkc + 1) * P],
                                       ones1[0:1, 0:1], start=True, stop=True),
                              waits=later(tglx, t_o1, pfree[1]))
                tece = prog.op("vector", lambda e, dkc=dkc, o=ec_sb, i=FB[1]:
                               e.tensor_copy(o[:, dkc:dkc + 1],
                                             i[:, dkc:dkc + 1]),
                               waits=later(tec, busy["ec"]))
                pwrite(1, tece)
                busy["glr"] = tec
                tS = prog.op("vector", lambda e, dkc=dkc, h=h, o=S_all,
                             c_=FB[2 + dkc], ec=ec_sb:
                             e.scalar_tensor_tensor(
                                 o[:, (h * 2 + dkc) * 512:(h * 2 + dkc + 1) * 512],
                                 o[:, (h * 2 + dkc) * 512:(h * 2 + dkc + 1) * 512],
                                 ec[:, dkc:dkc + 1], c_[:], ALU.mult, ALU.add),
                             waits=later(tc_, tece, tO, last_S))
                busy["ec"] = tS
                pwrite(2 + dkc, tS)
                last_S = tS
            busy["S"] = last_S
            tsq = prog.op("scalar", lambda e, o=o2_sb, i=FB[4]:
                          e.activation(o[:], i[:], AF.Square),
                          waits=later(tO, busy["o2"]))
            trm = None
            for s_ in range(4):
                wl = later(tsq, t_oc, pfree[1]) if s_ == 0 else ()
                trm = prog.op("tensor", lambda e, s_=s_, o=FB[1]:
                              e.matmul(o[0:1, 128:256],
                                       onescol[:], o2_sb[:, s_ * P:(s_ + 1) * P],
                                       start=(s_ == 0), stop=(s_ == 3)),
                              waits=wl)
            trr = prog.op("scalar", lambda e, o=rr_sb, i=FB[1], b=eps_col:
                          e.activation(o[:], i[0:1, 128:256], AF.Ln,
                                       bias=b[0:1, :], scale=1.0 / 512.0),
                          waits=later(trm, t_eps, busy["rr"]))
            pwrite(1, trr)
            trr2 = prog.op("scalar", lambda e, o=rr_sb:
                           e.activation(o[:], o[:], AF.Exp, scale=-0.5),
                           waits=later(trr))
            tbcr = prog.op("tensor", lambda e, o=FB[0]:
                           e.matmul(o[:, 0:P], ones1[:], rr_sb[:],
                                    start=True, stop=True),
                           waits=later(trr2, t_o1, pfree[0]))
            busy["rr"] = tbcr
            tbcre = prog.op("vector", lambda e, o=bcr_sb, i=FB[0]:
                            e.tensor_copy(o[:], i[:, 0:P]),
                            waits=later(tbcr, busy["bcr"]))
            pwrite(0, tbcre)
            tg2 = None
            for s_ in range(4):
                tg1 = prog.op("vector", lambda e, s_=s_, o=t1g_sb, i=FB[4],
                              g_=gam_sb:
                              e.scalar_tensor_tensor(o[:], i[:, s_ * P:(s_ + 1) * P],
                                                     g_[:, s_:s_ + 1], bcr_sb[:],
                                                     ALU.mult, ALU.mult),
                              waits=later(tbcre, tsq, t_gam, busy["t1g"]))
                busy["bcr"] = tg1
                ob = s_ % 2
                tg2 = prog.op("vector", lambda e, s_=s_, h=h, o=og_sb[ob], a=t1g_sb,
                              g_=gst[lb]:
                              e.tensor_tensor(o[:], a[:],
                                              g_[:, (h * 4 + s_) * P:
                                                 (h * 4 + s_ + 1) * P], ALU.mult),
                              waits=later(tg1, tgs, busy["og"][ob]))
                busy["t1g"] = tg2
                tst = prog.op("sync", lambda e, s_=s_, h=h, chk=chk, src=og_sb[ob]:
                              e.dma_start(out=og_scr[chk].rearrange(
                                  "p (c w) -> p c w", w=P)[:, h * 4 + s_, :],
                                  in_=src[:]),
                              waits=later(tg2), dma=True, dsem=f"og{ob}")
                busy["og"][ob] = tst
                if s_ == 3:
                    pwrite(4, tg1)
            busy["o2"] = trm
            last_gate = tg2
        busy["g"][lb] = later(tgc[0], tgc[1], tglr[0], tglr[1])[0]
        busy["v"][lb] = later(last_o_mm, last_contrib)[0]
        busy["gs"][lb] = last_gate

    prog.barrier()

    og_extra = [ctx.sb("op_act", [P, NCHUNK * P], BF16) for _ in range(2)]
    emit_out_proj(ctx, og_scr, wo_d, x_d, y_d, T, NCHUNK,
                  [FB[0], FB[1], FB[2], FB[3]], x_sb=xbufs,
                  act_sb=[gst[0], gst[1]] + og_extra, wt=wtile, ysb=evb)

    for s_, v in list(prog.cnt.items()):
        prog.wait("sync", s_, v)
    prog.build()
    return nc


def build_moe(T):
    I = 1408
    NI = I // P
    SI = 2816
    NSI = SI // P
    E = 8
    ngroups = T // 512

    nc = bass.Bass("TRN2", target_bir_lowering=False)
    x_d = nc.dram_tensor("x", [T, D], F32, kind="ExternalInput").ap()
    rw_d = nc.dram_tensor("rwt", [P, NCHUNK * E], F32, kind="ExternalInput").ap()
    gw_d = nc.dram_tensor("gw", [E, NI, P, NCHUNK, P], F32, kind="ExternalInput").ap()
    uw_d = nc.dram_tensor("uw", [E, NI, P, NCHUNK, P], F32, kind="ExternalInput").ap()
    dw_d = nc.dram_tensor("dw", [E, I, D], F32, kind="ExternalInput").ap()
    sg_d = nc.dram_tensor("sg", [NSI, P, NCHUNK, P], F32, kind="ExternalInput").ap()
    su_d = nc.dram_tensor("su", [NSI, P, NCHUNK, P], F32, kind="ExternalInput").ap()
    sd_d = nc.dram_tensor("sd", [SI, D], F32, kind="ExternalInput").ap()
    sgb_d = nc.dram_tensor("sgb2", [P, NSI], F32, kind="ExternalInput").ap()
    sub_d = nc.dram_tensor("sub2", [P, NSI], F32, kind="ExternalInput").ap()
    sdb_d = nc.dram_tensor("sdb_row", [1, D], F32, kind="ExternalInput").ap()
    id_d = nc.dram_tensor("ident", [P, P], F32, kind="ExternalInput").ap()
    y_d = nc.dram_tensor("y", [T, D], F32, kind="ExternalOutput").ap()

    ctx = Ctx(nc)
    prog = ctx.prog

    ident, t_id = emit_const_load(ctx, id_d, [P, P], BF16, "ident")
    rw_sb, t_rw = emit_const_load(ctx, rw_d, [P, NCHUNK * E], BF16, "rw")
    sgb_sb, t_sgb = emit_const_load(ctx, sgb_d, [P, NSI], F32, "sgb")
    sub_sb, t_sub = emit_const_load(ctx, sub_d, [P, NSI], F32, "sub")
    sdb_row, t_sdb = emit_const_load(ctx, sdb_d, [1, D], F32, "sdbr")
    lnb = make_ln_bufs(ctx, 512)
    lnb["ident"] = ident
    lnb["prev_xhat_reader"] = t_id

    ones1 = ctx.sb("ones1", [1, P], F32)
    t_o1 = prog.op("vector", lambda e, t=ones1: e.memset(t[:], 1.0))

    xhat_fm = ctx.sb("xhat_fm", [P, NCHUNK * 512], BF16)
    xbufs = [ctx.sb("xk", [P, D], F32) for _ in range(4)]
    wt = [ctx.sb("wt", [P, NCHUNK * P], BF16) for _ in range(2)]
    wt2 = [ctx.sb("wt2", [P, NCHUNK * P], BF16) for _ in range(2)]
    hgate = [ctx.sb("hgate", [P, 512], F32) for _ in range(2)]
    hup = [ctx.sb("hup", [P, 512], F32) for _ in range(2)]
    h_fm = ctx.sb("h_fm", [P, NSI * 512], BF16)
    acc = [ctx.sb("acc", [P, D], F32) for _ in range(4)]
    ysb = [ctx.sb("ysb", [P, 512], F32) for _ in range(2)]
    lg = ctx.sb("lg", [P, E], F32)
    rmax = ctx.sb("rmax", [P, 1], F32)
    p2c = ctx.sb("p2c", [P, 1], F32)
    msk = ctx.sb("msk", [P, E], F32)
    ce_all = ctx.sb("ce_all", [P, 4 * E], F32)
    sdb_bc = ctx.sb("sdb_bc", [P, D], F32)

    pg = ctx.ps("pg", [P, 512])
    pu = ctx.ps("pu", [P, 512])
    pdn = [ctx.ps("pdn", [P, 512]) for _ in range(4)]

    tprev = None
    for blk in range(4):
        tm = prog.op("tensor", lambda e, b=blk, o=pg:
                     e.matmul(o[:], ones1[:], sdb_row[0:1, b * 512:(b + 1) * 512],
                              start=True, stop=True),
                     waits=later(t_o1, t_sdb, tprev))
        tprev = prog.op("vector", lambda e, b=blk, o=sdb_bc, i=pg:
                        e.tensor_copy(o[:, b * 512:(b + 1) * 512], i[:]),
                        waits=later(tm))
    pg_busy = tprev

    xhat_busy = None
    x_busy4 = [None] * 4
    w_busy = [None, None]
    w2_busy = [None, None]
    hg_busy = [None, None]
    hu_busy = [None, None]
    pu_busy = None
    pdn_busy = [None] * 4
    h_busy = None
    acc_busy = [None] * 4
    y_busy = [None, None]
    lg_busy = None
    ce_busy = [None] * 4

    for g in range(ngroups):
        t_fm = None
        for t in range(4):
            guards = [x_busy4[t]]
            if t == 0:
                guards.append(xhat_busy)
            lnb["xsem"] = f"xk{t}"
            _, t_fm = emit_ln_xhat(ctx, x_d, g * 4 + t, lnb, xhat_fm, t * P,
                                   xbufs[t], x_write_guards=guards)
            x_busy4[t] = t_fm
        xhat_ready = t_fm

        for t in range(4):
            tmm = None
            for c in range(NCHUNK):
                wl = later(t_rw, xhat_ready, pg_busy, *(
                    [ce_busy[t]] if c == 0 else [])) if c == 0 else ()
                tmm = prog.op("tensor", lambda e, t=t, c=c, o=pg:
                              e.matmul(o[:, 0:E],
                                       xhat_fm[:, c * 512 + t * P:
                                               c * 512 + (t + 1) * P],
                                       rw_sb[:, c * E:(c + 1) * E],
                                       start=(c == 0), stop=(c == NCHUNK - 1)),
                              waits=wl)
            tv0 = prog.op("vector", lambda e, o=lg, i=pg:
                          e.tensor_copy(o[:], i[:, 0:E]),
                          waits=later(tmm, lg_busy))
            pg_busy = tv0
            tv1 = prog.op("vector", lambda e, o=rmax, i=lg:
                          e.reduce_max(o[:], i[:], axis=AX.X), waits=later(tv0))
            tv2 = prog.op("vector", lambda e, o=lg, m=rmax:
                          e.tensor_scalar(o[:], o[:], m[:], None, ALU.subtract),
                          waits=later(tv1))
            tv3 = prog.op("scalar", lambda e, o=lg:
                          e.activation(o[:], o[:], AF.Exp), waits=later(tv2))
            tv4 = prog.op("vector", lambda e, o=rmax, i=lg:
                          e.reduce_max(o[:], i[:], axis=AX.X), waits=later(tv3))
            tv5 = prog.op("vector", lambda e, o=msk, i=lg, m=rmax:
                          e.tensor_scalar(o[:], i[:], m[:], None, ALU.is_ge),
                          waits=later(tv4))
            tv6 = prog.op("vector", lambda e, o=msk, i=lg:
                          e.scalar_tensor_tensor(o[:], o[:], -1e30, i[:],
                                                 ALU.mult, ALU.add),
                          waits=later(tv5))
            tv7 = prog.op("vector", lambda e, o=p2c, i=msk:
                          e.reduce_max(o[:], i[:], axis=AX.X), waits=later(tv6))
            tv8 = prog.op("vector", lambda e, o=msk, i=lg, m=p2c:
                          e.tensor_scalar(o[:], i[:], m[:], None, ALU.is_ge),
                          waits=later(tv7))
            tv9 = prog.op("vector", lambda e, o=msk, i=lg:
                          e.tensor_tensor(o[:], o[:], i[:], ALU.mult),
                          waits=later(tv8))
            tva = prog.op("vector", lambda e, o=p2c, a=rmax:
                          e.tensor_tensor(o[:], o[:], a[:], ALU.add),
                          waits=later(tv9))
            tvb = prog.op("vector", lambda e, o=p2c:
                          e.reciprocal(o[:], o[:]), waits=later(tva))
            tvc = prog.op("vector", lambda e, o=ce_all, t=t, i=msk, m=p2c:
                          e.tensor_scalar(o[:, t * E:(t + 1) * E], i[:], m[:],
                                          None, ALU.mult),
                          waits=later(tvb, ce_busy[t]))
            ce_busy[t] = tvc
            lg_busy = tvc

        acc_init = []
        for t in range(4):
            ti = prog.op("vector", lambda e, o=acc[t]:
                         e.tensor_copy(o[:], sdb_bc[:]),
                         waits=later(tprev, acc_busy[t]))
            acc_init.append(ti)
            acc_busy[t] = ti

        def gu_pass(ni, gsrc, usrc, bias_g, bias_u, is_expert, eidx):
            nonlocal pu_busy, pg_busy, h_busy
            th = None
            for i in range(ni):
                bw = i % 2
                if is_expert:
                    sg_ = gsrc[eidx][i]
                    su_ = usrc[eidx][i]
                    tg = prog.op("gpsimd", lambda e, d=wt[bw], s=sg_:
                                 e.dma_start(out=d[:, 0:NCHUNK * P],
                                             in_=s.rearrange("p c o -> p (c o)")),
                                 waits=later(w_busy[bw]), dma=True, dsem=f"wg{bw}")
                    tu = prog.op("gpsimd", lambda e, d=wt2[bw], s=su_:
                                 e.dma_start(out=d[:, 0:NCHUNK * P],
                                             in_=s.rearrange("p c o -> p (c o)")),
                                 waits=later(w2_busy[bw]), dma=True, dsem=f"wu{bw}")
                else:
                    tg = prog.op("gpsimd", lambda e, d=wt[bw], s=gsrc[i]:
                                 e.dma_start(out=d[:, 0:NCHUNK * P],
                                             in_=s.rearrange("p c o -> p (c o)")),
                                 waits=later(w_busy[bw]), dma=True, dsem=f"wg{bw}")
                    tu = prog.op("gpsimd", lambda e, d=wt2[bw], s=usrc[i]:
                                 e.dma_start(out=d[:, 0:NCHUNK * P],
                                             in_=s.rearrange("p c o -> p (c o)")),
                                 waits=later(w2_busy[bw]), dma=True, dsem=f"wu{bw}")
                tmg = tmu = None
                for c in range(NCHUNK):
                    wl = later(tg, xhat_ready, pg_busy) if c == 0 else ()
                    tmg = prog.op("tensor", lambda e, w=wt[bw], c=c, o=pg:
                                  e.matmul(o[:], w[:, c * P:(c + 1) * P],
                                           xhat_fm[:, c * 512:(c + 1) * 512],
                                           start=(c == 0), stop=(c == NCHUNK - 1)),
                                  waits=wl)
                for c in range(NCHUNK):
                    wl = later(tu, pu_busy) if c == 0 else ()
                    tmu = prog.op("tensor", lambda e, w=wt2[bw], c=c, o=pu:
                                  e.matmul(o[:], w[:, c * P:(c + 1) * P],
                                           xhat_fm[:, c * 512:(c + 1) * 512],
                                           start=(c == 0), stop=(c == NCHUNK - 1)),
                                  waits=wl)
                w_busy[bw] = tmg
                w2_busy[bw] = tmu
                hb = i % 2
                if bias_g is not None:
                    ts_ = prog.op("scalar", lambda e, o=hgate[hb], i_=pg, b=bias_g,
                                  i=i:
                                  e.activation(o[:], i_[:], AF.Silu,
                                               bias=b[:, i:i + 1]),
                                  waits=later(tmg, hg_busy[hb], t_sgb))
                else:
                    ts_ = prog.op("scalar", lambda e, o=hgate[hb], i_=pg:
                                  e.activation(o[:], i_[:], AF.Silu),
                                  waits=later(tmg, hg_busy[hb]))
                pg_busy = ts_
                if bias_u is not None:
                    tsu = prog.op("vector", lambda e, o=hup[hb], i_=pu, b=bias_u,
                                  i=i:
                                  e.tensor_scalar(o[:], i_[:], b[:, i:i + 1], None,
                                                  ALU.add),
                                  waits=later(tmu, hu_busy[hb], t_sub))
                    pu_busy = tsu
                    th = prog.op("vector", lambda e, hg=hgate[hb], up=hup[hb],
                                 o=h_fm, i=i:
                                 e.tensor_tensor(o[:, i * 512:(i + 1) * 512],
                                                 hg[:], up[:], ALU.mult),
                                 waits=later(ts_, tsu,
                                             h_busy if i == 0 else None))
                    hu_busy[hb] = th
                else:
                    th = prog.op("vector", lambda e, hg=hgate[hb], up=pu, o=h_fm,
                                 i=i:
                                 e.tensor_tensor(o[:, i * 512:(i + 1) * 512],
                                                 hg[:], up[:], ALU.mult),
                                 waits=later(ts_, tmu,
                                             h_busy if i == 0 else None))
                    pu_busy = th
                hg_busy[hb] = th
            return th

        def down_pass(ni, dsrc, is_expert, eidx, final):
            nonlocal h_busy, y_busy
            h_ready = None
            for n in range(4):
                last_mm = None
                for i in range(ni):
                    bw = i % 2
                    if is_expert:
                        s_ = dsrc[eidx][i * P:(i + 1) * P, n * 512:(n + 1) * 512]
                    else:
                        s_ = dsrc[i * P:(i + 1) * P, n * 512:(n + 1) * 512]
                    td = prog.op("gpsimd", lambda e, d=wt[bw], s=s_:
                                 e.dma_start(out=d[:, 0:512], in_=s),
                                 waits=later(w_busy[bw]), dma=True, dsem=f"wg{bw}")
                    for t in range(4):
                        wl = ()
                        if t == 0:
                            wl = later(td, h_ready_tok,
                                       *(pdn_busy if i == 0 else []))
                        last_mm = prog.op("tensor",
                                          lambda e, w=wt[bw], t=t, i=i, o=pdn[t]:
                                          e.matmul(o[:],
                                                   h_fm[:, i * 512 + t * P:
                                                        i * 512 + (t + 1) * P],
                                                   w[:, 0:512], start=(i == 0),
                                                   stop=(i == ni - 1)),
                                          waits=wl)
                    w_busy[bw] = last_mm
                for t in range(4):
                    if not final:
                        tev = prog.op("vector",
                                      lambda e, t=t, n=n, ps=pdn[t], o=acc[t],
                                      ce=ce_all, ei=eidx:
                                      e.scalar_tensor_tensor(
                                          o[:, n * 512:(n + 1) * 512], ps[:],
                                          ce[:, t * E + ei:t * E + ei + 1],
                                          o[:, n * 512:(n + 1) * 512],
                                          ALU.mult, ALU.add),
                                      waits=later(last_mm, acc_busy[t],
                                                  ce_busy[t]))
                        acc_busy[t] = tev
                        pdn_busy[t] = tev
                    else:
                        k = (n * 4 + t) % 2
                        tev = prog.op("vector",
                                      lambda e, t=t, n=n, ps=pdn[t], o=ysb[k],
                                      a=acc[t]:
                                      e.tensor_tensor(o[:], ps[:],
                                                      a[:, n * 512:(n + 1) * 512],
                                                      ALU.add),
                                      waits=later(last_mm, acc_busy[t],
                                                  y_busy[k]))
                        tev = prog.op("vector",
                                      lambda e, o=ysb[k], x=xbufs[t], n=n:
                                      e.tensor_tensor(o[:], o[:],
                                                      x[:, n * 512:(n + 1) * 512],
                                                      ALU.add),
                                      waits=later(tev))
                        pdn_busy[t] = tev
                        acc_busy[t] = tev
                        if n == 3:
                            x_busy4[t] = tev
                        tst = prog.op("sync",
                                      lambda e, yb=ysb[k], g=g, t=t, n=n:
                                      e.dma_start(
                                          out=y_d[(g * 4 + t) * P:
                                                  (g * 4 + t + 1) * P,
                                                  n * 512:(n + 1) * 512],
                                          in_=yb[:]),
                                      waits=later(tev), dma=True, dsem=f"ys{k}")
                        y_busy[k] = tst
            h_busy = last_mm
            return last_mm

        for eidx in range(E):
            h_ready_tok = gu_pass(NI, gw_d, uw_d, None, None, True, eidx)
            down_pass(NI, dw_d, True, eidx, False)
        for t in range(4):
            ce_busy[t] = acc_busy[t]
        h_ready_tok = gu_pass(NSI, sg_d, su_d, sgb_sb, sub_sb, False, 0)
        down_pass(NSI, sd_d, False, 0, True)
        xhat_busy = w_busy[0] if (w_busy[1] is None or
                                  (w_busy[0] and w_busy[0][1] > w_busy[1][1])) \
            else w_busy[1]

    for s_, v in list(prog.cnt.items()):
        prog.wait("sync", s_, v)
    prog.build()
    return nc


def _np(x):
    return np.ascontiguousarray(np.asarray(x, dtype=np.float32))


IDENT = np.eye(P, dtype=np.float32)


def tile_w_fm(w):
    din, dout = w.shape
    t = w.reshape(din // P, P, dout // P, P)
    return _np(t.transpose(2, 1, 0, 3))


def run_ffn_layer(x_flat, w_gate_up, w_down, nc_cache={}):
    INTER = 8192
    key = ("ffn", x_flat.shape[0] // 8)
    if key not in nc_cache:
        nc_cache[key] = build_ffn(x_flat.shape[0] // 8)
    nc = nc_cache[key]
    wg = tile_w_fm(_np(w_gate_up[:INTER]).T)
    wu = tile_w_fm(_np(w_gate_up[INTER:]).T)
    wd = _np(w_down).T.copy()
    Tc = x_flat.shape[0] // 8
    in_maps = [{"x": _np(x_flat[c * Tc:(c + 1) * Tc]),
                "wg": wg, "wu": wu, "wd": _np(wd), "ident": IDENT}
               for c in range(8)]
    res = run_bass_kernel_spmd(nc, in_maps, core_ids=list(range(8)))
    return np.concatenate([res.results[c]["y"] for c in range(8)], axis=0)


def make_sel(dh):
    hpc = P // dh
    sel = np.zeros((P, hpc), dtype=np.float32)
    for j in range(hpc):
        sel[j * dh:(j + 1) * dh, j] = 1.0
    return sel


def make_dmask():
    r = np.arange(P)[:, None]
    q = np.arange(512)[None, :]
    return np.stack([(r + 128 * rr <= q).astype(np.float32) for rr in range(4)])


def run_attn_layer(x_seqs, cfg, weights, nc_cache={}):
    H, G, dh, has_bias = cfg
    T = 4096
    key = ("attn", H, G, dh, has_bias)
    if key not in nc_cache:
        nc_cache[key] = build_attn(T, H, G, dh, has_bias)
    nc = nc_cache[key]
    base = {"wq": _np(weights["wq"]), "wk": _np(weights["wk"]),
            "wv": _np(weights["wv"]), "wo": _np(weights["wo"]),
            "ident": IDENT, "sel": make_sel(dh), "dmask": make_dmask()}
    if has_bias:
        NMQ = H * dh // P
        NMK = G * dh // P
        base["bq2"] = _np(weights["bq"]).reshape(NMQ, P).T.copy()
        base["bk2"] = _np(weights["bk"]).reshape(NMK, P).T.copy()
        base["bv_row"] = _np(weights["bv"]).reshape(1, -1)
        base["bo_row"] = _np(weights["bo"]).reshape(1, -1)
    in_maps = [dict(base, x=_np(x_seqs[c // 2])) for c in range(8)]
    res = run_bass_kernel_spmd(nc, in_maps, core_ids=list(range(8)))
    out = np.empty((4, 4096, D), np.float32)
    for c in range(8):
        half = c % 2
        out[c // 2, half * 2048:(half + 1) * 2048] = \
            res.results[c]["y"][half * 2048:(half + 1) * 2048]
    return out


def run_gla_layer(x_seqs, weights, nc_cache={}):
    T = 4096
    key = ("gla",)
    if key not in nc_cache:
        nc_cache[key] = build_gla(T)
    nc = nc_cache[key]
    wgk1t = _np(weights["Wgk1"]).reshape(NCHUNK, P, 16).transpose(1, 0, 2)
    base = {"wq": _np(weights["Wq"]), "wk": _np(weights["Wk"]),
            "wv": _np(weights["Wv"]), "wg": _np(weights["Wg"]),
            "wo": _np(weights["Wo"]), "wgk1t": _np(wgk1t).reshape(P, NCHUNK * 16),
            "wgk2": _np(weights["Wgk2"]),
            "bgk2_row": _np(weights["bgk2"]).reshape(1, 1024),
            "gamma2": _np(weights["g_norm_weight"]).reshape(4, P).T.copy(),
            "ident": IDENT, "ut": np.triu(np.ones((P, P), np.float32))}
    in_maps = [dict(base, x=_np(x_seqs[c // 2])) for c in range(8)]
    res = run_bass_kernel_spmd(nc, in_maps, core_ids=list(range(8)))
    out = np.empty((4, 4096, D), np.float32)
    for c in range(8):
        half = c % 2
        out[c // 2, half * 2048:(half + 1) * 2048] = \
            res.results[c]["y"][half * 2048:(half + 1) * 2048]
    return out


def run_moe_layer(x_flat, W, nc_cache={}):
    Tc = x_flat.shape[0] // 8
    key = ("moe", Tc)
    if key not in nc_cache:
        nc_cache[key] = build_moe(Tc)
    nc = nc_cache[key]
    NI, NSI, E = 11, 22, 8
    rwt = _np(W["router_w"]).T.reshape(NCHUNK, P, E).transpose(1, 0, 2)
    gw = _np(W["gate_w"]).reshape(E, NCHUNK, P, NI, P).transpose(0, 3, 2, 1, 4)
    uw = _np(W["up_w"]).reshape(E, NCHUNK, P, NI, P).transpose(0, 3, 2, 1, 4)
    sg = _np(W["s_gate_w"]).reshape(NCHUNK, P, NSI, P).transpose(2, 1, 0, 3)
    su = _np(W["s_up_w"]).reshape(NCHUNK, P, NSI, P).transpose(2, 1, 0, 3)
    base = {"rwt": _np(rwt).reshape(P, NCHUNK * E), "gw": _np(gw), "uw": _np(uw),
            "dw": _np(W["down_w"]), "sg": _np(sg), "su": _np(su),
            "sd": _np(W["s_down_w"]),
            "sgb2": _np(W["s_gate_b"]).reshape(NSI, P).T.copy(),
            "sub2": _np(W["s_up_b"]).reshape(NSI, P).T.copy(),
            "sdb_row": _np(W["s_down_b"]).reshape(1, D), "ident": IDENT}
    in_maps = [dict(base, x=_np(x_flat[c * Tc:(c + 1) * Tc])) for c in range(8)]
    res = run_bass_kernel_spmd(nc, in_maps, core_ids=list(range(8)))
    return np.concatenate([res.results[c]["y"] for c in range(8)], axis=0)


def _aot_compile(nc, n_cores=8):
    import jax
    from concourse import bass2jax
    from jax.sharding import Mesh, PartitionSpec
    from jax.experimental.shard_map import shard_map

    bass2jax.install_neuronx_cc_hook()
    partition_name = nc.partition_id_tensor.name if nc.partition_id_tensor else None
    in_names, out_names, out_avals = [], [], []
    zero_shapes = []
    in_shapes = {}
    for alloc in nc.m.functions[0].allocations:
        if not isinstance(alloc, mybir.MemoryLocationSet):
            continue
        name = alloc.memorylocations[0].name
        if alloc.kind == "ExternalInput":
            in_shapes[name] = (tuple(alloc.tensor_shape), mybir.dt.np(alloc.dtype))
            if name != partition_name:
                in_names.append(name)
        elif alloc.kind == "ExternalOutput":
            shape = tuple(alloc.tensor_shape)
            dtype = mybir.dt.np(alloc.dtype)
            out_names.append(name)
            out_avals.append(jax.core.ShapedArray(shape, dtype))
            zero_shapes.append((shape, dtype))
    n_params = len(in_names)
    all_names = list(in_names) + list(out_names)
    if partition_name is not None:
        all_names.append(partition_name)

    def _body(*args):
        operands = list(args)
        if partition_name is not None:
            operands.append(bass2jax.partition_id_tensor())
        outs = bass2jax._bass_exec_p.bind(
            *operands, out_avals=tuple(out_avals), in_names=tuple(all_names),
            out_names=tuple(out_names), lowering_input_output_aliases=(),
            sim_require_finite=True, sim_require_nnan=True, nc=nc)
        return tuple(outs)

    devices = jax.devices()[:n_cores]
    mesh = Mesh(np.asarray(devices), ("core",))
    n_outs = len(out_avals)
    in_specs = (PartitionSpec("core"),) * (n_params + n_outs)
    out_specs = (PartitionSpec("core"),) * n_outs
    donate = tuple(range(n_params, n_params + n_outs))
    f = jax.jit(
        shard_map(_body, mesh=mesh, in_specs=in_specs, out_specs=out_specs,
                  check_rep=False),
        donate_argnums=donate, keep_unused=True)
    args = []
    for nm in in_names:
        shp, dt = in_shapes[nm]
        args.append(jax.ShapeDtypeStruct((n_cores * shp[0],) + tuple(shp[1:]), dt))
    for shp, dt in zero_shapes:
        args.append(jax.ShapeDtypeStruct((n_cores * shp[0],) + tuple(shp[1:]), dt))
    f.lower(*args).compile()


def _precompile_all(ncs):
    import concurrent.futures as cf

    def one(nc):
        try:
            _aot_compile(nc)
        except Exception:
            pass

    with cf.ThreadPoolExecutor(max_workers=6) as ex:
        list(ex.map(one, ncs))


def kernel(**inputs):
    g = lambda k: _np(inputs[k])
    x4 = g("L0_attn_a_hidden_states")

    gla_w = {"Wq": g("L0_attn_a_Wq"), "Wk": g("L0_attn_a_Wk"),
             "Wv": g("L0_attn_a_Wv"), "Wgk1": g("L0_attn_a_Wgk1"),
             "Wgk2": g("L0_attn_a_Wgk2"), "bgk2": g("L0_attn_a_bgk2"),
             "Wg": g("L0_attn_a_Wg"), "g_norm_weight": g("L0_attn_a_g_norm_weight"),
             "Wo": g("L0_attn_a_Wo")}
    l0b_w = {"wq": g("L0_attn_b_Wq"), "wk": g("L0_attn_b_Wk"),
             "wv": g("L0_attn_b_Wv"), "wo": g("L0_attn_b_Wo")}
    ffn_gu = g("L0_ffn_w_gate_up")
    ffn_d = g("L0_ffn_w_down")
    l1a_w = {"wq": _np(g("L1_attn_a_Wq").T), "wk": _np(g("L1_attn_a_Wk").T),
             "wv": _np(g("L1_attn_a_Wv").T), "wo": _np(g("L1_attn_a_Wo").T),
             "bq": g("L1_attn_a_bq"), "bk": g("L1_attn_a_bk"),
             "bv": g("L1_attn_a_bv"), "bo": g("L1_attn_a_bo")}
    wqkvT = _np(g("L1_attn_b_w_qkv").T)
    l1b_w = {"wq": _np(wqkvT[:, 0:2048]), "wk": _np(wqkvT[:, 2048:4096]),
             "wv": _np(wqkvT[:, 4096:6144]), "wo": _np(g("L1_attn_b_w_o").T)}
    moe_w = {"router_w": g("L1_ffn_router_w"), "gate_w": g("L1_ffn_gate_w"),
             "up_w": g("L1_ffn_up_w"), "down_w": g("L1_ffn_down_w"),
             "s_gate_w": g("L1_ffn_s_gate_w"), "s_gate_b": g("L1_ffn_s_gate_b"),
             "s_up_w": g("L1_ffn_s_up_w"), "s_up_b": g("L1_ffn_s_up_b"),
             "s_down_w": g("L1_ffn_s_down_w"), "s_down_b": g("L1_ffn_s_down_b")}

    import inspect
    gcache = inspect.signature(run_gla_layer).parameters["nc_cache"].default
    acache = inspect.signature(run_attn_layer).parameters["nc_cache"].default
    fcache = inspect.signature(run_ffn_layer).parameters["nc_cache"].default
    mcache = inspect.signature(run_moe_layer).parameters["nc_cache"].default
    T4 = 4096
    if ("gla",) not in gcache:
        gcache[("gla",)] = build_gla(T4)
    for cfg in ((32, 8, 64, False), (16, 16, 128, True), (16, 16, 128, False)):
        key = ("attn",) + cfg
        if key not in acache:
            acache[key] = build_attn(T4, *cfg)
    if ("ffn", 2048) not in fcache:
        fcache[("ffn", 2048)] = build_ffn(2048)
    if ("moe", 2048) not in mcache:
        mcache[("moe", 2048)] = build_moe(2048)
    _precompile_all([gcache[("gla",)],
                     acache[("attn", 32, 8, 64, False)],
                     acache[("attn", 16, 16, 128, True)],
                     acache[("attn", 16, 16, 128, False)],
                     fcache[("ffn", 2048)],
                     mcache[("moe", 2048)]])

    x4 = run_gla_layer(x4, gla_w)
    xf = run_ffn_layer(x4.reshape(-1, D), ffn_gu, ffn_d)
    x4 = run_attn_layer(xf.reshape(4, 4096, D), (32, 8, 64, False), l0b_w)
    xf = run_ffn_layer(x4.reshape(-1, D), ffn_gu, ffn_d)
    x4 = run_attn_layer(xf.reshape(4, 4096, D), (16, 16, 128, True), l1a_w)
    xf = run_moe_layer(x4.reshape(-1, D), moe_w)
    x4 = run_attn_layer(xf.reshape(4, 4096, D), (16, 16, 128, False), l1b_w)
    xf = run_moe_layer(x4.reshape(-1, D), moe_w)
    return xf.reshape(4, 4096, D)


# version 05
# speedup vs baseline: 1.0224x; 1.0224x over previous
import numpy as np
import ml_dtypes
import concourse.bass as bass
import concourse.mybir as mybir
from concourse.bass_utils import run_bass_kernel_spmd

F32 = mybir.dt.float32
BF16 = mybir.dt.bfloat16
AF = mybir.ActivationFunctionType
ALU = mybir.AluOpType
AX = mybir.AxisListType

P = 128
D = 2048
NCHUNK = D // P
LN_EPS = 1e-5

ENGS = ("sync", "tensor", "scalar", "vector", "gpsimd")


class Prog:

    def __init__(self, nc):
        self.nc = nc
        self.q = {e: [] for e in ENGS}
        self.sems = {}
        self.cnt = {}
        self.last_wait = {e: {} for e in ENGS}
        self.engsem = {"tensor": "T", "vector": "V", "scalar": "S",
                       "sync": "sdma", "gpsimd": "gdma"}

    def sem(self, name):
        if name not in self.sems:
            self.sems[name] = self.nc.alloc_semaphore(name)
            self.cnt[name] = 0
        return self.sems[name]

    def wait(self, eng, sem_name, val):
        if val <= 0:
            return
        if self.last_wait[eng].get(sem_name, -1) >= val:
            return
        self.last_wait[eng][sem_name] = val
        h = self.sem(sem_name)
        self.q[eng].append(lambda e, h=h, v=val: e.wait_ge(h, v))

    PARANOID = set()

    def op(self, eng, fn, waits=(), dma=False, inc=True, dsem=None):
        own = self.engsem[eng]
        if eng in Prog.PARANOID:
            for s, v in list(self.cnt.items()):
                self.wait(eng, s, v)
        for tok in waits:
            if tok is None:
                continue
            s, v = tok
            self.wait(eng, s, v)
        if not inc:
            self.q[eng].append(fn)
            return None
        if dma:
            assert dsem is not None, "DMA ops need a per-buffer dsem"
            sname = dsem
            k = 16
        else:
            sname = own
            k = 1
        h = self.sem(sname)
        self.cnt[sname] += k
        tgt = self.cnt[sname]
        self.q[eng].append(lambda e, fn=fn, h=h, k=k: fn(e).then_inc(h, k))
        return (sname, tgt)

    def barrier(self):
        snapshot = dict(self.cnt)
        for e in ENGS:
            for s, v in snapshot.items():
                self.wait(e, s, v)

    def build(self):
        with self.nc.Block() as block:
            for e in ENGS:
                items = self.q[e]
                if not items:
                    continue
                getattr(block, e)(lambda eng, items=items: [f(eng) for f in items])


class Ctx:
    def __init__(self, nc):
        self.nc = nc
        self.prog = Prog(nc)
        self.n = 0

    def _nm(self, base):
        self.n += 1
        return f"{base}_{self.n}"

    def sb(self, name, shape, dt):
        return self.nc.alloc_sbuf_tensor(self._nm(name), shape, dt)

    def ps(self, name, shape, dt=F32):
        return self.nc.alloc_psum_tensor(self._nm(name), shape, dt)

    def const_col(self, value):
        if not hasattr(self, "_consts"):
            self._consts = {}
        if value not in self._consts:
            t = self.sb("cc", [P, 1], F32)
            tok = self.prog.op("vector", lambda e, t=t, v=value:
                               e.memset(t[:], v))
            self._consts[value] = (t, tok)
        return self._consts[value]


def later(*toks):
    best = {}
    for t in toks:
        if t is None:
            continue
        s, v = t
        if best.get(s, -1) < v:
            best[s] = v
    return [(s, v) for s, v in best.items()]


def make_ln_bufs(ctx, fm_width):
    return {
        "sumc": ctx.sb("ln_sum", [P, 1], F32),
        "mean": ctx.sb("ln_mean", [P, 1], F32),
        "xc": ctx.sb("ln_xc", [P, D], F32),
        "sq": ctx.sb("ln_sq", [P, D], F32),
        "var": ctx.sb("ln_var", [P, 1], F32),
        "rstd": ctx.sb("ln_rstd", [P, 1], F32),
        "xhat_tm": ctx.sb("ln_xhat", [P, D], BF16),
        "tbanks": [ctx.ps("ln_tb", [P, 4 * P], BF16) for _ in range(2)],
        "tb_free": [None, None],
        "ident": None,
        "W": fm_width,
        "xsem": "xk",
        "prev_mean_reader": None,
        "prev_sumc_reader": None,
        "prev_xc_reader": None,
        "prev_xc_reader_v": None,
        "prev_var_reader": None,
        "prev_rstd_reader": None,
        "prev_xhat_reader": None,
    }


def emit_ln_xhat(ctx, x_dram, tile_idx, b, out_fm, out_tok_off, x_keep,
                 x_write_guards=()):
    p = ctx.prog
    W = b["W"]

    t_ld = p.op("sync", lambda e, x=x_keep, s=x_dram, i=tile_idx:
                e.dma_start(out=x[:], in_=s[i * P:(i + 1) * P, :]),
                waits=later(*x_write_guards), dma=True, dsem=b["xsem"])
    t_sum = p.op("vector", lambda e, x=x_keep, o=b["sumc"]:
                 e.reduce_sum(o[:], x[:], axis=AX.X),
                 waits=later(t_ld, b["prev_sumc_reader"]))
    t_mean = p.op("scalar", lambda e, i=b["sumc"], o=b["mean"]:
                  e.mul(o[:], i[:], 1.0 / D),
                  waits=later(t_sum, b["prev_mean_reader"]))
    b["prev_sumc_reader"] = t_mean
    t_xc = p.op("vector", lambda e, x=x_keep, m=b["mean"], o=b["xc"]:
                e.tensor_scalar(o[:], x[:], m[:], None, ALU.subtract),
                waits=later(t_mean, b["prev_xc_reader"], b["prev_xc_reader_v"]))
    b["prev_mean_reader"] = t_xc
    t_sq = p.op("scalar", lambda e, i=b["xc"], o=b["sq"], a=b["var"]:
                e.activation(o[:], i[:], AF.Square, accum_out=a[:]),
                waits=later(t_xc, b["prev_var_reader"]))
    b["prev_xc_reader"] = t_sq
    eps_col, eps_tok = ctx.const_col(LN_EPS)
    t_sd = p.op("scalar", lambda e, i=b["var"], o=b["rstd"], bc=eps_col:
                e.activation(o[:], i[:], AF.Sqrt, bias=bc[:], scale=1.0 / D),
                waits=later(t_sq, b["prev_rstd_reader"], eps_tok))
    b["prev_var_reader"] = t_sd
    t_r = p.op("vector", lambda e, i=b["rstd"]: e.reciprocal(i[:], i[:]),
               waits=later(t_sd))
    t_xh = p.op("vector", lambda e, x=b["xc"], r=b["rstd"], o=b["xhat_tm"]:
                e.tensor_scalar(o[:], x[:], r[:], None, ALU.mult),
                waits=later(t_r, b["prev_xhat_reader"]))
    b["prev_rstd_reader"] = t_xh
    b["prev_xc_reader_v"] = t_xh

    last_copy = None
    last_mm = None
    for rnd in range(4):
        bank = b["tbanks"][rnd % 2]
        tmm = None
        for j in range(4):
            c = rnd * 4 + j
            wl = later(t_xh, b["tb_free"][rnd % 2]) if j == 0 else ()
            tmm = p.op("tensor", lambda e, bk=bank, j=j, c=c, xh=b["xhat_tm"],
                       idn=b["ident"]:
                       e.transpose(bk[:, j * P:(j + 1) * P],
                                   xh[:, c * P:(c + 1) * P], idn[:]),
                       waits=wl)
        c0 = rnd * 4
        dst = out_fm[:, :].rearrange("p (c w) -> p c w", w=W)[
            :, c0:c0 + 4, out_tok_off:out_tok_off + P]
        srcv = bank[:, 0:4 * P].rearrange("p (c w) -> p c w", w=P)
        tcp = p.op("vector", lambda e, s=srcv, d=dst: e.tensor_copy(d, s),
                   waits=later(tmm))
        b["tb_free"][rnd % 2] = tcp
        last_copy = tcp
        last_mm = tmm
    b["prev_xhat_reader"] = last_mm
    return t_ld, last_copy


def emit_const_load(ctx, dram_ap, shape, dt, name):
    t = ctx.sb(name, shape, dt)
    eng = "gpsimd" if dt == BF16 else "sync"
    tok = ctx.prog.op(eng, lambda e, t=t, s=dram_ap: e.dma_start(out=t[:], in_=s),
                      dma=True, dsem="cst")
    return t, tok


def build_ffn(T):
    INTER = 8192
    NI = INTER // P
    G = 512
    ngroups = T // G

    nc = bass.Bass("TRN2", target_bir_lowering=False)
    x_d = nc.dram_tensor("x", [T, D], F32, kind="ExternalInput").ap()
    wg_d = nc.dram_tensor("wg", [NI, P, NCHUNK, P], F32, kind="ExternalInput").ap()
    wu_d = nc.dram_tensor("wu", [NI, P, NCHUNK, P], F32, kind="ExternalInput").ap()
    wd_d = nc.dram_tensor("wd", [INTER, D], F32, kind="ExternalInput").ap()
    id_d = nc.dram_tensor("ident", [P, P], F32, kind="ExternalInput").ap()
    y_d = nc.dram_tensor("y", [T, D], F32, kind="ExternalOutput").ap()

    ctx = Ctx(nc)
    prog = ctx.prog

    ident, t_id = emit_const_load(ctx, id_d, [P, P], BF16, "ident")
    lnb = make_ln_bufs(ctx, G)
    lnb["ident"] = ident
    lnb["prev_xhat_reader"] = t_id

    xhat_fm = ctx.sb("xhat_fm", [P, NCHUNK * G], BF16)
    x_keep = [ctx.sb("xkeep", [P, D], F32) for _ in range(4)]
    h_fm = ctx.sb("h_fm", [P, NI * G], BF16)
    hgate = [ctx.sb("hgate", [P, G], F32) for _ in range(2)]
    wgt = [ctx.sb("wgt", [P, NCHUNK * P], BF16) for _ in range(2)]
    wut = [ctx.sb("wut", [P, NCHUNK * P], BF16) for _ in range(2)]
    wdt = [ctx.sb("wdt", [P, G], BF16) for _ in range(2)]
    ysb = [ctx.sb("ysb", [P, G], F32) for _ in range(2)]

    pg = ctx.ps("pgate", [P, G])
    pu = ctx.ps("pup", [P, G])
    pdn = [ctx.ps("pdn", [P, G]) for _ in range(4)]

    xhat_busy = None
    h_busy = None
    x_busy = [None] * 4
    y_busy = [None, None]
    pg_busy = None
    pu_busy = None
    hg_busy = [None, None]
    pdn_busy = [None] * 4
    wg_busy = [None, None]
    wu_busy = [None, None]
    wd_busy = [None, None]

    for g in range(ngroups):
        t_fm = None
        for t in range(4):
            guards = [x_busy[t]]
            if t == 0:
                guards.append(xhat_busy)
            lnb["xsem"] = f"xk{t}"
            _, t_fm = emit_ln_xhat(ctx, x_d, g * 4 + t, lnb, xhat_fm, t * P,
                                   x_keep[t], x_write_guards=guards)
        xhat_ready = t_fm

        t_h = None
        for i in range(NI):
            bw = i % 2
            tg = prog.op("gpsimd", lambda e, d=wgt[bw], s=wg_d[i]:
                         e.dma_start(out=d[:], in_=s.rearrange("p c o -> p (c o)")),
                         waits=later(wg_busy[bw]), dma=True, dsem=f"wg{bw}")
            tu = prog.op("gpsimd", lambda e, d=wut[bw], s=wu_d[i]:
                         e.dma_start(out=d[:], in_=s.rearrange("p c o -> p (c o)")),
                         waits=later(wu_busy[bw]), dma=True, dsem=f"wu{bw}")
            tmmg = tmmu = None
            for c in range(NCHUNK):
                wl = later(tg, xhat_ready, pg_busy) if c == 0 else ()
                tmmg = prog.op("tensor", lambda e, w=wgt[bw], c=c, o=pg:
                               e.matmul(o[:], w[:, c * P:(c + 1) * P],
                                        xhat_fm[:, c * G:(c + 1) * G],
                                        start=(c == 0), stop=(c == NCHUNK - 1)),
                               waits=wl)
            for c in range(NCHUNK):
                wl = later(tu, pu_busy) if c == 0 else ()
                tmmu = prog.op("tensor", lambda e, w=wut[bw], c=c, o=pu:
                               e.matmul(o[:], w[:, c * P:(c + 1) * P],
                                        xhat_fm[:, c * G:(c + 1) * G],
                                        start=(c == 0), stop=(c == NCHUNK - 1)),
                               waits=wl)
            wg_busy[bw] = tmmg
            wu_busy[bw] = tmmu
            hb = i % 2
            t_silu = prog.op("scalar", lambda e, o=hgate[hb], i_=pg:
                             e.activation(o[:], i_[:], AF.Silu),
                             waits=later(tmmg, hg_busy[hb]))
            pg_busy = t_silu
            t_h = prog.op("vector", lambda e, hg=hgate[hb], up=pu, o=h_fm, i=i:
                          e.tensor_tensor(o[:, i * G:(i + 1) * G], hg[:], up[:],
                                          ALU.mult),
                          waits=later(t_silu, tmmu, h_busy if i == 0 else None))
            hg_busy[hb] = t_h
            pu_busy = t_h
        h_ready = t_h
        xhat_busy = wu_busy[(NI - 1) % 2]

        last_mm = None
        for n in range(4):
            for i in range(NI):
                bw = i % 2
                src_d = wd_d[i * P:(i + 1) * P, n * G:(n + 1) * G]
                td = prog.op("gpsimd", lambda e, d=wdt[bw], s=src_d:
                             e.dma_start(out=d[:], in_=s),
                             waits=later(wd_busy[bw]), dma=True, dsem=f"wd{bw}")
                for t in range(4):
                    wl = ()
                    if t == 0:
                        wl = later(td, h_ready,
                                   *(pdn_busy if i == 0 else [None]))
                    last_mm = prog.op("tensor",
                                      lambda e, w=wdt[bw], t=t, i=i, o=pdn[t]:
                                      e.matmul(o[:],
                                               h_fm[:, i * G + t * P:i * G + (t + 1) * P],
                                               w[:], start=(i == 0),
                                               stop=(i == NI - 1)),
                                      waits=wl)
                wd_busy[bw] = last_mm
            for t in range(4):
                k = (n * 4 + t) % 2
                yb = ysb[k]
                tev = prog.op("vector", lambda e, o=yb, n=n, ps=pdn[t], x=x_keep[t]:
                              e.tensor_tensor(o[:], ps[:],
                                              x[:, n * G:(n + 1) * G], ALU.add),
                              waits=later(last_mm, y_busy[k]))
                pdn_busy[t] = tev
                x_busy[t] = tev
                tst = prog.op("sync", lambda e, yb=yb, g=g, t=t, n=n:
                              e.dma_start(out=y_d[(g * 4 + t) * P:(g * 4 + t + 1) * P,
                                                  n * G:(n + 1) * G], in_=yb[:]),
                              waits=later(tev), dma=True, dsem=f"ys{k}")
                y_busy[k] = tst
        h_busy = last_mm

    for s, v in list(prog.cnt.items()):
        prog.wait("sync", s, v)
    prog.build()
    return nc


def emit_out_proj(ctx, act_scr, w_dram, x_dram, y_dram, T, NC, banks, bo_bc=None,
                  bo_tok=None, x_sb=None, act_sb=None, wt=None, ysb=None,
                  w_bf16=False):
    p = ctx.prog
    if act_sb is None:
        act_sb = [ctx.sb("op_act", [P, NC * P], BF16) for _ in range(4)]
    if x_sb is None:
        x_sb = [ctx.sb("op_x", [P, D], F32) for _ in range(4)]
    if wt is None:
        wt = [ctx.sb("op_w", [P, 512], BF16) for _ in range(2)]
    if ysb is None:
        ysb = [ctx.sb("op_y", [P, 512], F32) for _ in range(2)]
    act_busy = [None] * 4
    x_busy = [None] * 4
    w_busy = [None, None]
    y_busy = [None, None]
    bank_busy = [None] * 4
    for gg in range(T // 512):
        a_toks = []
        for t in range(4):
            tile = gg * 4 + t
            a_toks.append(p.op("sync", lambda e, d=act_sb[t], s=act_scr, i=tile:
                               e.dma_start(out=d[:], in_=s[i]),
                               waits=later(act_busy[t]), dma=True, dsem=f"opa{t}"))
            p.op("sync", lambda e, d=x_sb[t], s=x_dram, i=tile:
                 e.dma_start(out=d[:], in_=s[i * P:(i + 1) * P, :]),
                 waits=later(x_busy[t]), dma=True, dsem=f"opx{t}")
            x_busy[t] = (f"opx{t}", p.cnt[f"opx{t}"])
        for n in range(4):
            last_mm = None
            for c in range(NC):
                bw = c % 2
                tw = p.op("sync" if w_bf16 else "gpsimd",
                          lambda e, d=wt[bw], s=w_dram, c=c, n=n:
                          e.dma_start(out=d[:], in_=s[c * P:(c + 1) * P,
                                                      n * 512:(n + 1) * 512]),
                          waits=later(w_busy[bw]), dma=True, dsem=f"opw{bw}")
                for t in range(4):
                    wl = ()
                    if t == 0:
                        wl = later(tw, *a_toks, *(bank_busy if c == 0 else []))
                    last_mm = p.op("tensor",
                                   lambda e, w=wt[bw], t=t, c=c, o=banks[t]:
                                   e.matmul(o[:], act_sb[t][:, c * P:(c + 1) * P],
                                            w[:], start=(c == 0),
                                            stop=(c == NC - 1)), waits=wl)
                w_busy[bw] = last_mm
            for t in range(4):
                k = (n * 4 + t) % 2
                tev = p.op("vector", lambda e, o=ysb[k], ps=banks[t], x=x_sb[t], n=n:
                           e.tensor_tensor(o[:], ps[:], x[:, n * 512:(n + 1) * 512],
                                           ALU.add),
                           waits=later(last_mm, y_busy[k], x_busy[t]))
                if bo_bc is not None:
                    tev = p.op("vector", lambda e, o=ysb[k], b=bo_bc, n=n:
                               e.tensor_tensor(o[:], o[:], b[:, n * 512:(n + 1) * 512],
                                               ALU.add), waits=later(tev, bo_tok))
                bank_busy[t] = tev
                if n == 3:
                    x_busy[t] = tev
                    act_busy[t] = tev
                tst = p.op("sync", lambda e, yb=ysb[k], gg=gg, t=t, n=n:
                           e.dma_start(out=y_dram[(gg * 4 + t) * P:(gg * 4 + t + 1) * P,
                                                  n * 512:(n + 1) * 512], in_=yb[:]),
                           waits=later(tev), dma=True, dsem=f"opy{k}")
                y_busy[k] = tst


def build_attn(T, H, G, dh, has_bias):
    HD = H * dh
    GD = G * dh
    NMQ = HD // P
    NMK = GD // P
    NC = HD // P
    hpc = P // dh
    ktiles = T // P
    ngroups = T // 512
    s = float(dh) ** -0.5

    nc = bass.Bass("TRN2", target_bir_lowering=False)
    x_d = nc.dram_tensor("x", [T, D], F32, kind="ExternalInput").ap()
    wq_d = nc.dram_tensor("wq", [D, HD], F32, kind="ExternalInput").ap()
    wk_d = nc.dram_tensor("wk", [D, GD], F32, kind="ExternalInput").ap()
    wv_d = nc.dram_tensor("wv", [D, GD], F32, kind="ExternalInput").ap()
    wo_d = nc.dram_tensor("wo", [HD, D], F32, kind="ExternalInput").ap()
    id_d = nc.dram_tensor("ident", [P, P], F32, kind="ExternalInput").ap()
    sel_d = nc.dram_tensor("sel", [P, hpc], F32, kind="ExternalInput").ap()
    dm_d = nc.dram_tensor("dmask", [4, P, 512], F32, kind="ExternalInput").ap()
    if has_bias:
        bq_d = nc.dram_tensor("bq2", [P, NMQ], F32, kind="ExternalInput").ap()
        bk_d = nc.dram_tensor("bk2", [P, NMK], F32, kind="ExternalInput").ap()
        bv_d = nc.dram_tensor("bv_row", [1, GD], F32, kind="ExternalInput").ap()
        bo_d = nc.dram_tensor("bo_row", [1, D], F32, kind="ExternalInput").ap()
    y_d = nc.dram_tensor("y", [T, D], F32, kind="ExternalOutput").ap()

    wqb_d = nc.dram_tensor("wqb", [D, HD], BF16, kind="Internal").ap()
    wkb_d = nc.dram_tensor("wkb", [D, GD], BF16, kind="Internal").ap()
    wvb_d = nc.dram_tensor("wvb", [D, GD], BF16, kind="Internal").ap()
    wob_d = nc.dram_tensor("wob", [HD, D], BF16, kind="Internal").ap()
    q_scr = nc.dram_tensor("q_scr", [H, dh, T], BF16, kind="Internal").ap()
    k_scr = nc.dram_tensor("k_scr", [G, dh, T], BF16, kind="Internal").ap()
    v_scr = nc.dram_tensor("v_scr", [G, P, ktiles * dh], BF16, kind="Internal").ap()
    a_scr = nc.dram_tensor("a_scr", [ktiles, P, NC * P], BF16, kind="Internal").ap()

    ctx = Ctx(nc)
    prog = ctx.prog

    for j, (srcw, dstw) in enumerate(((wq_d, wqb_d), (wk_d, wkb_d),
                                      (wv_d, wvb_d), (wo_d, wob_d))):
        ctx.prog.op("gpsimd", lambda e, s=srcw, d=dstw:
                    e.dma_start(out=d, in_=s), dma=True, dsem=f"wc{j % 2}")
    wcast_tok = [("wc0", ctx.prog.cnt["wc0"]), ("wc1", ctx.prog.cnt["wc1"])]
    ident, t_id = emit_const_load(ctx, id_d, [P, P], BF16, "ident")
    sel_sb, t_sel = emit_const_load(ctx, sel_d, [P, hpc], BF16, "sel")
    dm_sb = ctx.sb("dmask", [P, 4 * 512], BF16)
    t_dm = prog.op("gpsimd", lambda e, d=dm_sb, s=dm_d:
                   e.dma_start(out=d[:].rearrange("p (r w) -> p r w", w=512), in_=s.rearrange("r p w -> p r w")),
                   dma=True, dsem="cst2")
    lnb = make_ln_bufs(ctx, 512)
    lnb["ident"] = ident
    lnb["prev_xhat_reader"] = t_id

    ones1 = ctx.sb("ones1", [1, P], F32)
    t_o1 = prog.op("vector", lambda e, t=ones1: e.memset(t[:], 1.0))
    negones1 = ctx.sb("neg1", [1, P], F32)
    t_n1 = prog.op("vector", lambda e, t=negones1: e.memset(t[:], -1.0))
    onescol_bf = ctx.sb("onescol", [P, 1], BF16)
    t_oc = prog.op("vector", lambda e, t=onescol_bf: e.memset(t[:], 1.0))
    mqrow = ctx.sb("mqrow", [1, H], F32)
    mkrow = ctx.sb("mkrow", [1, G], F32)
    mq_tok = prog.op("vector", lambda e, t=mqrow: e.memset(t[:], 0.0))
    mk_tok = prog.op("vector", lambda e, t=mkrow: e.memset(t[:], 0.0))

    xhat_fm = ctx.sb("xhat_fm", [P, NCHUNK * 512], BF16)
    xbufs = [ctx.sb("xk", [P, D], F32) for _ in range(4)]
    wtile = [ctx.sb("wtile", [P, 512], BF16) for _ in range(2)]
    qsb = [ctx.sb("qsb", [P, 512], BF16) for _ in range(2)]
    q2sb = ctx.sb("q2sb", [P, 512], BF16)
    nrow = ctx.sb("nrow", [1, 1], F32)

    banks = [ctx.ps("pa", [P, 512]) for _ in range(4)]
    pb = ctx.ps("pb", [P, 512])
    pnorm = ctx.ps("pnorm", [1, 512])

    state = {
        "w_busy": [None, None],
        "q_busy": [[], []],
        "q2_busy": None,
        "pnorm_busy": None,
        "nrow_busy": None,
        "bank_busy": [None] * 4,
        "pb_busy": None,
    }

    if has_bias:
        bq_sb, t_bq = emit_const_load(ctx, bq_d, [P, NMQ], F32, "bq")
        bk_sb, t_bk = emit_const_load(ctx, bk_d, [P, NMK], F32, "bk")
        bv_row_sb, t_bvr = emit_const_load(ctx, bv_d, [1, GD], F32, "bvr")
        bo_row_sb, t_bor = emit_const_load(ctx, bo_d, [1, D], F32, "bor")
        bv_bc = ctx.sb("bv_bc", [P, GD], F32)
        bo_bc = ctx.sb("bo_bc", [P, D], F32)
        tprev = None
        for dst, row_sb, row_tok, width in ((bv_bc, bv_row_sb, t_bvr, GD),
                                            (bo_bc, bo_row_sb, t_bor, D)):
            for blk in range(width // 512):
                tm = prog.op("tensor", lambda e, b=blk, o=pb, r=row_sb:
                             e.matmul(o[:], ones1[:], r[:, b * 512:(b + 1) * 512],
                                      start=True, stop=True),
                             waits=later(t_o1, row_tok, tprev))
                tprev = prog.op("vector", lambda e, b=blk, o=dst, i=pb:
                                e.tensor_copy(o[:, b * 512:(b + 1) * 512], i[:]),
                                waits=later(tm))
        bo_tok = tprev
        state["pb_busy"] = tprev
    else:
        bq_sb = bk_sb = bv_bc = bo_bc = bo_tok = None

    def proj_fm(w_src, nm, bias_sb, scr, row_sb, row_tok, xhat_ready, t0):
        st = state
        toks = row_tok
        for mb in range((nm + 3) // 4):
            ms = list(range(mb * 4, min(mb * 4 + 4, nm)))
            last_mm = [None] * len(ms)
            for c in range(NCHUNK):
                bw = c % 2
                tw = prog.op("sync",
                             lambda e, d=wtile[bw], s=w_src, c=c, mb=mb,
                             w=len(ms) * P:
                             e.dma_start(out=d[:, 0:w],
                                         in_=s[c * P:(c + 1) * P,
                                               mb * 512:mb * 512 + w]),
                             waits=later(st["w_busy"][bw], *wcast_tok), dma=True,
                             dsem=f"wt{bw}")
                for jj in range(len(ms)):
                    wl = ()
                    if jj == 0:
                        wl = later(tw, xhat_ready,
                                   *(st["bank_busy"] if c == 0 else []))
                    last_mm[jj] = prog.op("tensor",
                                          lambda e, w=wtile[bw], jj=jj, c=c,
                                          o=banks[jj]:
                                          e.matmul(o[:], w[:, jj * P:(jj + 1) * P],
                                                   xhat_fm[:, c * 512:(c + 1) * 512],
                                                   start=(c == 0),
                                                   stop=(c == NCHUNK - 1)),
                                          waits=wl)
                st["w_busy"][bw] = last_mm[len(ms) - 1]
            for jj, m in enumerate(ms):
                qk = m % 2
                if bias_sb is not None:
                    tev = prog.op("vector", lambda e, o=qsb[qk], i=banks[jj],
                                  b=bias_sb, m=m:
                                  e.tensor_scalar(o[:], i[:], b[:, m:m + 1], None,
                                                  ALU.add),
                                  waits=later(last_mm[jj], *st["q_busy"][qk]))
                else:
                    tev = prog.op("scalar", lambda e, o=qsb[qk], i=banks[jj]:
                                  e.activation(o[:], i[:], AF.Copy),
                                  waits=later(last_mm[jj], *st["q_busy"][qk]))
                st["bank_busy"][jj] = tev
                st_toks = []
                for j in range(hpc):
                    h = m * hpc + j
                    st_toks.append(prog.op("sync",
                                   lambda e, src=qsb[qk], j=j, h=h, scr=scr, t0=t0:
                                   e.dma_start(out=scr[h][:, t0:t0 + 512],
                                               in_=src[j * dh:(j + 1) * dh, :]),
                                   waits=later(tev), dma=True, dsem=f"qs{qk}"))
                tsq = prog.op("scalar", lambda e, o=q2sb, i=qsb[qk]:
                              e.activation(o[:], i[:], AF.Square),
                              waits=later(tev, st["q2_busy"]))
                st["q_busy"][qk] = [tsq, st_toks[-1]]
                for j in range(hpc):
                    h = m * hpc + j
                    tn = prog.op("tensor", lambda e, j=j, o=pnorm:
                                 e.matmul(o[:], sel_sb[:, j:j + 1], q2sb[:],
                                          start=True, stop=True),
                                 waits=later(tsq, t_sel, st["pnorm_busy"]))
                    st["q2_busy"] = tn
                    tr = prog.op("vector", lambda e, o=nrow, i=pnorm:
                                 e.reduce_max(o[:], i[:], axis=AX.X),
                                 waits=later(tn, st["nrow_busy"]))
                    st["pnorm_busy"] = tr
                    tmx = prog.op("vector", lambda e, o=row_sb, h=h, i=nrow:
                                  e.tensor_tensor(o[0:1, h:h + 1], o[0:1, h:h + 1],
                                                  i[:], ALU.max),
                                  waits=later(tr, toks))
                    st["nrow_busy"] = tmx
                    toks = tmx
        return toks

    x_busy4 = [None] * 4
    xhat_busy = None
    for g in range(ngroups):
        t_fm = None
        for t in range(4):
            guards = [x_busy4[t]]
            if t == 0:
                guards.append(xhat_busy)
            lnb["xsem"] = f"xk{t}"
            _, t_fm = emit_ln_xhat(ctx, x_d, g * 4 + t, lnb, xhat_fm, t * P,
                                   xbufs[t], x_write_guards=guards)
            x_busy4[t] = t_fm
        xhat_ready = t_fm

        mq_tok = proj_fm(wqb_d, NMQ, bq_sb, q_scr, mqrow, mq_tok, xhat_ready,
                         g * 512)
        mk_tok = proj_fm(wkb_d, NMK, bk_sb, k_scr, mkrow, mk_tok, xhat_ready,
                         g * 512)

        for vb in range(GD // 512 if GD >= 512 else 1):
            w = min(512, GD)
            last_mm = None
            for c in range(NCHUNK):
                bw = c % 2
                tw = prog.op("sync",
                             lambda e, d=wtile[bw], s=wvb_d, c=c, vb=vb, w=w:
                             e.dma_start(out=d[:, 0:w],
                                         in_=s[c * P:(c + 1) * P,
                                               vb * 512:vb * 512 + w]),
                             waits=later(state["w_busy"][bw], *wcast_tok),
                             dma=True, dsem=f"wt{bw}")
                for t in range(4):
                    wl = ()
                    if t == 0:
                        wl = later(tw, xhat_ready,
                                   *(state["bank_busy"] if c == 0 else []))
                    last_mm = prog.op("tensor",
                                      lambda e, wti=wtile[bw], t=t, c=c,
                                      o=banks[t], w=w:
                                      e.matmul(o[:, 0:w],
                                               xhat_fm[:, c * 512 + t * P:
                                                       c * 512 + (t + 1) * P],
                                               wti[:, 0:w], start=(c == 0),
                                               stop=(c == NCHUNK - 1)),
                                      waits=wl)
                state["w_busy"][bw] = last_mm
            for t in range(4):
                qk = t % 2
                tile = g * 4 + t
                if has_bias:
                    tev = prog.op("vector",
                                  lambda e, o=qsb[qk], i=banks[t], vb=vb, w=w:
                                  e.tensor_tensor(o[:, 0:w], i[:, 0:w],
                                                  bv_bc[:, vb * 512:vb * 512 + w],
                                                  ALU.add),
                                  waits=later(last_mm, state["pb_busy"],
                                              *state["q_busy"][qk]))
                else:
                    tev = prog.op("vector",
                                  lambda e, o=qsb[qk], i=banks[t], w=w:
                                  e.tensor_copy(o[:, 0:w], i[:, 0:w]),
                                  waits=later(last_mm, *state["q_busy"][qk]))
                state["bank_busy"][t] = tev
                st_toks = []
                for kv in range(w // dh):
                    kvh = vb * (512 // dh) + kv
                    st_toks.append(prog.op("sync",
                                   lambda e, src=qsb[qk], kv=kv, kvh=kvh, tile=tile:
                                   e.dma_start(out=v_scr[kvh][:, tile * dh:
                                                              (tile + 1) * dh],
                                               in_=src[:, kv * dh:(kv + 1) * dh]),
                                   waits=later(tev), dma=True, dsem=f"qs{qk}"))
                state["q_busy"][qk] = [st_toks[-1]]
        xhat_busy = last_mm

    prog.barrier()

    ksb = [ctx.sb("ksb", [dh, T], BF16) for _ in range(2)]
    vsb = [ctx.sb("vsb", [P, ktiles * dh], BF16) for _ in range(2)]
    qblk = [ctx.sb("qblk", [dh, 512], BF16) for _ in range(2)]
    ptb = [ctx.sb("ptb", [P, 512], BF16) for _ in range(2)]
    zr = ctx.sb("zr", [1, 512], F32)
    bcz = ctx.sb("bcz", [P, 512], F32)
    asb = [ctx.sb("asb", [P, 512], BF16) for _ in range(2)]
    bias_col = ctx.sb("bias_col", [P, 1], F32)
    mh = ctx.sb("mh", [1, 1], F32)
    lnc, t_lnc = ctx.const_col(float(np.log(1.01 * s)))

    st_banks = banks[0:2]
    pv_bank = banks[2]
    z_bank = pnorm
    bc_bank = banks[3]

    kv_busy = [None, None]
    q_blk_busy = [None, None]
    pt_busy = [None, None]
    st_bank_busy = [None, None]
    pv_busy = None
    z_busy = None
    bc_busy = None
    asb_busy = [None, None]
    bias_busy = None
    mh_busy = None
    zr_busy = None
    bcz_busy = None

    for kvh in range(G):
        kb = kvh % 2
        tk = prog.op("sync", lambda e, d=ksb[kb], s=k_scr, kvh=kvh:
                     e.dma_start(out=d[:], in_=s[kvh]),
                     waits=later(kv_busy[kb]), dma=True, dsem=f"kv{kb}")
        tv = prog.op("sync", lambda e, d=vsb[kb], s=v_scr, kvh=kvh:
                     e.dma_start(out=d[:], in_=s[kvh]),
                     waits=later(kv_busy[kb]), dma=True, dsem=f"vv{kb}")
        last_kv_reader = None
        for hq in range(H // G):
            h = hq * G + kvh

            t1 = prog.op("vector", lambda e, o=mh, h=h, kvh=kvh:
                         e.tensor_tensor(o[:], mqrow[0:1, h:h + 1],
                                         mkrow[0:1, kvh:kvh + 1], ALU.mult),
                         waits=later(mq_tok, mk_tok, mh_busy))
            t2 = prog.op("scalar", lambda e, o=mh: e.activation(o[:], o[:], AF.Ln),
                         waits=later(t1))
            t3 = prog.op("scalar", lambda e, o=mh, b=lnc:
                         e.activation(o[:], o[:], AF.Exp, bias=b[0:1, :], scale=0.5),
                         waits=later(t2, t_lnc))
            t4 = prog.op("tensor", lambda e, o=bc_bank:
                         e.matmul(o[:, 0:1], negones1[:], mh[:], start=True,
                                  stop=True),
                         waits=later(t3, t_n1, bc_busy))
            mh_busy = t4
            t5 = prog.op("vector", lambda e, o=bias_col, i=bc_bank:
                         e.tensor_copy(o[:], i[:, 0:1]),
                         waits=later(t4, bias_busy))
            bc_busy = t5
            tbias = t5
            for qb in range(T // 512):
                qkb = qb % 2
                tq = prog.op("sync", lambda e, d=qblk[qkb], s=q_scr, h=h, qb=qb:
                             e.dma_start(out=d[:],
                                         in_=s[h][:, qb * 512:(qb + 1) * 512]),
                             waits=later(q_blk_busy[qkb]), dma=True,
                             dsem=f"qb{qkb}")
                ktmax = 4 * qb + 4
                t_pv = None
                t_z = None
                for kt in range(ktmax):
                    sb_ = kt % 2
                    tmm = prog.op("tensor", lambda e, o=st_banks[sb_], kt=kt,
                                  k=ksb[kb], q=qblk[qkb]:
                                  e.matmul(o[:], k[:, kt * P:(kt + 1) * P], q[:],
                                           start=True, stop=True),
                                  waits=later(tk, tq, st_bank_busy[sb_]))
                    texp = prog.op("scalar", lambda e, o=ptb[sb_], i=st_banks[sb_],
                                   b=bias_col:
                                   e.activation(o[:], i[:], AF.Exp, bias=b[:],
                                                scale=s),
                                   waits=later(tmm, tbias, pt_busy[sb_]))
                    bias_busy = texp
                    st_bank_busy[sb_] = texp
                    tp = texp
                    if kt >= 4 * qb:
                        r = kt - 4 * qb
                        tp = prog.op("vector", lambda e, o=ptb[sb_], r=r:
                                     e.tensor_tensor(o[:], o[:],
                                                     dm_sb[:, r * 512:(r + 1) * 512],
                                                     ALU.mult),
                                     waits=later(texp, t_dm))
                    t_pv = prog.op("tensor", lambda e, o=pv_bank, kt=kt, v=vsb[kb],
                                   pt=ptb[sb_], first=(kt == 0),
                                   last=(kt == ktmax - 1):
                                   e.matmul(o[0:dh, :], v[:, kt * dh:(kt + 1) * dh],
                                            pt[:], start=first, stop=last),
                                   waits=later(tp, tv,
                                               pv_busy if kt == 0 else None))
                    t_z = prog.op("tensor", lambda e, o=z_bank, kt=kt, pt=ptb[sb_],
                                  first=(kt == 0), last=(kt == ktmax - 1):
                                  e.matmul(o[:], onescol_bf[:], pt[:], start=first,
                                           stop=last),
                                  waits=later(t_oc, z_busy if kt == 0 else None))
                    pt_busy[sb_] = t_z
                last_kv_reader = t_z
                q_blk_busy[qkb] = t_z
                trc = prog.op("vector", lambda e, o=zr, i=z_bank:
                              e.reciprocal(o[:], i[:]), waits=later(t_z, zr_busy))
                z_busy = trc
                tbc = prog.op("tensor", lambda e, o=bc_bank:
                              e.matmul(o[:], ones1[:], zr[:], start=True, stop=True),
                              waits=later(trc, t_o1, bc_busy))
                zr_busy = tbc
                tcp = prog.op("vector", lambda e, o=bcz, i=bc_bank:
                              e.tensor_copy(o[:], i[:]), waits=later(tbc, bcz_busy))
                bc_busy = tcp
                ab = qb % 2
                tev = prog.op("vector", lambda e, o=asb[ab], i=pv_bank, b=bcz:
                              e.tensor_tensor(o[0:dh, :], i[0:dh, :], b[0:dh, :],
                                              ALU.mult),
                              waits=later(tcp, t_pv, asb_busy[ab]))
                bcz_busy = tev
                pv_busy = tev
                st_toks = []
                poff = (h * dh) % P
                cc = (h * dh) // P
                for j in range(4):
                    tile = qb * 4 + j
                    st_toks.append(prog.op("sync",
                                   lambda e, src=asb[ab], j=j, tile=tile,
                                   poff=poff, cc=cc:
                                   e.dma_start(
                                       out=a_scr[tile].rearrange(
                                           "p (c w) -> p c w", w=P)[
                                           poff:poff + dh, cc, :],
                                       in_=src[0:dh, j * P:(j + 1) * P]),
                                   waits=later(tev), dma=True, dsem=f"as{ab}"))
                asb_busy[ab] = st_toks[-1]
        kv_busy[kb] = last_kv_reader

    prog.barrier()

    emit_out_proj(ctx, a_scr, wob_d, x_d, y_d, T, NC, banks, bo_bc=bo_bc,
                  bo_tok=bo_tok, x_sb=xbufs, w_bf16=True)

    for s_, v in list(prog.cnt.items()):
        prog.wait("sync", s_, v)
    prog.build()
    return nc


def build_gla(T):
    NH = 4
    nchunks = T // P
    ngroups = T // 512
    lnS = float(np.log(1.0 / 16.0))

    nc = bass.Bass("TRN2", target_bir_lowering=False)
    x_d = nc.dram_tensor("x", [T, D], F32, kind="ExternalInput").ap()
    wq_d = nc.dram_tensor("wq", [D, 1024], F32, kind="ExternalInput").ap()
    wk_d = nc.dram_tensor("wk", [D, 1024], F32, kind="ExternalInput").ap()
    wv_d = nc.dram_tensor("wv", [D, 2048], F32, kind="ExternalInput").ap()
    wg_d = nc.dram_tensor("wg", [D, 2048], F32, kind="ExternalInput").ap()
    wo_d = nc.dram_tensor("wo", [D, D], F32, kind="ExternalInput").ap()
    wgk1_d = nc.dram_tensor("wgk1t", [P, NCHUNK * 16], F32, kind="ExternalInput").ap()
    wgk2_d = nc.dram_tensor("wgk2", [16, 1024], F32, kind="ExternalInput").ap()
    bgk2_d = nc.dram_tensor("bgk2_row", [1, 1024], F32, kind="ExternalInput").ap()
    gam_d = nc.dram_tensor("gamma2", [P, 4], F32, kind="ExternalInput").ap()
    id_d = nc.dram_tensor("ident", [P, P], F32, kind="ExternalInput").ap()
    ut_d = nc.dram_tensor("ut", [P, P], F32, kind="ExternalInput").ap()
    y_d = nc.dram_tensor("y", [T, D], F32, kind="ExternalOutput").ap()

    q_scr = nc.dram_tensor("q_scr", [T, 1024], F32, kind="Internal").ap()
    k_scr = nc.dram_tensor("k_scr", [T, 1024], F32, kind="Internal").ap()
    v_scr = nc.dram_tensor("v_scr", [T, 2048], F32, kind="Internal").ap()
    gk_scr = nc.dram_tensor("gk_scr", [T, 1024], F32, kind="Internal").ap()
    gs_scr = nc.dram_tensor("gs_scr", [D, T], BF16, kind="Internal").ap()
    og_scr = nc.dram_tensor("og_scr", [T // P, P, NCHUNK * P], BF16,
                            kind="Internal").ap()

    ctx = Ctx(nc)
    prog = ctx.prog

    identb, t_idb = emit_const_load(ctx, id_d, [P, P], BF16, "identb")
    identf, t_idf = emit_const_load(ctx, id_d, [P, P], F32, "identf")
    ut_sb, t_ut = emit_const_load(ctx, ut_d, [P, P], F32, "ut")
    gam_sb, t_gam = emit_const_load(ctx, gam_d, [P, 4], F32, "gam")
    wgk1_sb, t_w1 = emit_const_load(ctx, wgk1_d, [P, NCHUNK * 16], BF16, "wgk1")
    wgk2_sb, t_w2 = emit_const_load(ctx, wgk2_d, [16, 1024], BF16, "wgk2")
    bgk2_row, t_bg = emit_const_load(ctx, bgk2_d, [1, 1024], F32, "bgk2r")

    lnb = make_ln_bufs(ctx, 512)
    lnb["ident"] = identb
    lnb["prev_xhat_reader"] = t_idb

    ones1 = ctx.sb("ones1", [1, P], F32)
    t_o1 = prog.op("vector", lambda e, t=ones1: e.memset(t[:], 1.0))
    onescol = ctx.sb("onescol", [P, 1], F32)
    t_oc = prog.op("vector", lambda e, t=onescol: e.memset(t[:], 1.0))
    one_col, t_one = ctx.const_col(1.0)
    eps_col, t_eps = ctx.const_col(LN_EPS)
    lnS_col, t_lnS = ctx.const_col(lnS)

    xhat_fm = ctx.sb("xhat_fm", [P, NCHUNK * 512], BF16)
    xbufs = [ctx.sb("xk", [P, D], F32) for _ in range(4)]
    wtile = [ctx.sb("wtile", [P, 512], BF16) for _ in range(2)]
    evb = [ctx.sb("evb", [P, 512], F32) for _ in range(2)]
    evbh = [ctx.sb("evbh", [P, 512], BF16) for _ in range(2)]
    u_sb = ctx.sb("u_sb", [16, 512], BF16)
    zt_sb = ctx.sb("zt", [P, 512], F32)
    t3_sb = ctx.sb("t3", [P, 512], F32)
    bgk_bc = ctx.sb("bgk_bc", [P, 1024], F32)

    FB = [ctx.ps("fb", [P, 512]) for _ in range(6)]
    pfree = [None] * 6

    def pwrite(idx, tok):
        pfree[idx] = tok

    st = {"w_busy": [None, None], "ev_busy": [[], []], "evh_busy": [[], []],
          "bank_busy": [None] * 4, "u_busy": None, "z_busy": None,
          "t3_busy": None, "zt_r": None}

    tprev = None
    for blk in range(2):
        tm = prog.op("tensor", lambda e, b=blk, o=FB[5]:
                     e.matmul(o[:], ones1[:], bgk2_row[0:1, b * 512:(b + 1) * 512],
                              start=True, stop=True),
                     waits=later(t_o1, t_bg, tprev))
        tprev = prog.op("vector", lambda e, b=blk, o=bgk_bc, i=FB[5]:
                        e.tensor_copy(o[:, b * 512:(b + 1) * 512], i[:]),
                        waits=later(tm))
    pwrite(5, tprev)

    def proj_tm(w_src, wout, scr, xhat_ready, t0, sem_pfx):
        for nb in range(wout // 512):
            last_mm = None
            for c in range(NCHUNK):
                bw = c % 2
                tw = prog.op("gpsimd",
                             lambda e, d=wtile[bw], s=w_src, c=c, nb=nb:
                             e.dma_start(out=d[:], in_=s[c * P:(c + 1) * P,
                                                         nb * 512:(nb + 1) * 512]),
                             waits=later(st["w_busy"][bw]), dma=True, dsem=f"wt{bw}")
                for t in range(4):
                    wl = ()
                    if t == 0:
                        wl = later(tw, xhat_ready,
                                   *(st["bank_busy"] if c == 0 else []))
                    last_mm = prog.op("tensor",
                                      lambda e, w=wtile[bw], t=t, c=c, o=FB[t]:
                                      e.matmul(o[:],
                                               xhat_fm[:, c * 512 + t * P:
                                                       c * 512 + (t + 1) * P],
                                               w[:], start=(c == 0),
                                               stop=(c == NCHUNK - 1)),
                                      waits=wl)
                st["w_busy"][bw] = last_mm
            for t in range(4):
                eb = t % 2
                tev = prog.op("vector", lambda e, o=evb[eb], i=FB[t]:
                              e.tensor_copy(o[:], i[:]),
                              waits=later(last_mm, *st["ev_busy"][eb]))
                st["bank_busy"][t] = tev
                tst = prog.op("sync", lambda e, src=evb[eb], t0=t0, t=t, nb=nb,
                              scr=scr:
                              e.dma_start(out=scr[t0 + t * P:t0 + (t + 1) * P,
                                                  nb * 512:(nb + 1) * 512],
                                          in_=src[:]),
                              waits=later(tev), dma=True, dsem=f"ev{eb}")
                st["ev_busy"][eb] = [tst]

    def proj_fm_silu(w_src, scr, xhat_ready, t0):
        for mb in range(4):
            last_mm = [None] * 4
            for c in range(NCHUNK):
                bw = c % 2
                tw = prog.op("gpsimd",
                             lambda e, d=wtile[bw], s=w_src, c=c, mb=mb:
                             e.dma_start(out=d[:], in_=s[c * P:(c + 1) * P,
                                                         mb * 512:(mb + 1) * 512]),
                             waits=later(st["w_busy"][bw]), dma=True, dsem=f"wt{bw}")
                for jj in range(4):
                    wl = ()
                    if jj == 0:
                        wl = later(tw, xhat_ready,
                                   *(st["bank_busy"] if c == 0 else []))
                    last_mm[jj] = prog.op("tensor",
                                          lambda e, w=wtile[bw], jj=jj, c=c,
                                          o=FB[jj]:
                                          e.matmul(o[:], w[:, jj * P:(jj + 1) * P],
                                                   xhat_fm[:, c * 512:(c + 1) * 512],
                                                   start=(c == 0),
                                                   stop=(c == NCHUNK - 1)),
                                          waits=wl)
                st["w_busy"][bw] = last_mm[3]
            for jj in range(4):
                m = mb * 4 + jj
                eb = jj % 2
                tev = prog.op("scalar", lambda e, o=evbh[eb], i=FB[jj]:
                              e.activation(o[:], i[:], AF.Silu),
                              waits=later(last_mm[jj], *st["evh_busy"][eb]))
                st["bank_busy"][jj] = tev
                tst = prog.op("sync", lambda e, src=evbh[eb], m=m, t0=t0, scr=scr:
                              e.dma_start(out=scr[m * P:(m + 1) * P, t0:t0 + 512],
                                          in_=src[:]),
                              waits=later(tev), dma=True, dsem=f"eh{eb}")
                st["evh_busy"][eb] = [tst]

    x_busy4 = [None] * 4
    xhat_busy = None
    for g in range(ngroups):
        t_fm = None
        for t in range(4):
            guards = [x_busy4[t]]
            if t == 0:
                guards.append(xhat_busy)
            lnb["xsem"] = f"xk{t}"
            _, t_fm = emit_ln_xhat(ctx, x_d, g * 4 + t, lnb, xhat_fm, t * P,
                                   xbufs[t], x_write_guards=guards)
            x_busy4[t] = t_fm
        xhat_ready = t_fm
        t0 = g * 512

        proj_tm(wq_d, 1024, q_scr, xhat_ready, t0, "q")
        proj_tm(wk_d, 1024, k_scr, xhat_ready, t0, "k")
        proj_tm(wv_d, 2048, v_scr, xhat_ready, t0, "v")

        tu = None
        for c in range(NCHUNK):
            wl = later(t_w1, xhat_ready, pfree[5]) if c == 0 else ()
            tu = prog.op("tensor", lambda e, c=c, o=FB[5]:
                         e.matmul(o[0:16, :], wgk1_sb[:, c * 16:(c + 1) * 16],
                                  xhat_fm[:, c * 512:(c + 1) * 512],
                                  start=(c == 0), stop=(c == NCHUNK - 1)),
                         waits=wl)
        tue = prog.op("vector", lambda e, o=u_sb, i=FB[5]:
                      e.tensor_copy(o[:], i[0:16, :]),
                      waits=later(tu, st["u_busy"]))
        pwrite(5, tue)
        for t in range(4):
            for nb in range(2):
                bk = 4 if (t + nb) % 2 == 0 else 5
                tz = prog.op("tensor", lambda e, t=t, nb=nb, o=FB[bk]:
                             e.matmul(o[:], u_sb[:, t * P:(t + 1) * P],
                                      wgk2_sb[:, nb * 512:(nb + 1) * 512],
                                      start=True, stop=True),
                             waits=later(tue, t_w2, pfree[bk]))
                ta = prog.op("vector", lambda e, o=zt_sb, i=FB[bk], nb=nb:
                             e.tensor_tensor(o[:], i[:],
                                             bgk_bc[:, nb * 512:(nb + 1) * 512],
                                             ALU.add),
                             waits=later(tz, tprev, st["z_busy"]))
                pwrite(bk, ta)
                tb = prog.op("scalar", lambda e, o=t3_sb, i=zt_sb:
                             e.activation(o[:], i[:], AF.Abs),
                             waits=later(ta, st["t3_busy"]))
                tc = prog.op("scalar", lambda e, o=t3_sb:
                             e.activation(o[:], o[:], AF.Exp, scale=-1.0),
                             waits=later(tb))
                td = prog.op("scalar", lambda e, o=t3_sb, b=one_col:
                             e.activation(o[:], o[:], AF.Ln, bias=b[:]),
                             waits=later(tc, t_one))
                te = prog.op("vector", lambda e, o=zt_sb, i=zt_sb:
                             e.tensor_scalar(o[:], i[:], 0.0, 1.0 / 16.0,
                                             ALU.min, ALU.mult),
                             waits=later(td))
                tf = prog.op("vector", lambda e, o=zt_sb, a=t3_sb:
                             e.scalar_tensor_tensor(o[:], a[:], -1.0 / 16.0, o[:],
                                                    ALU.mult, ALU.add),
                             waits=later(te))
                st["t3_busy"] = tf
                tst = prog.op("sync", lambda e, src=zt_sb, t=t, nb=nb, t0=t0:
                              e.dma_start(out=gk_scr[t0 + t * P:t0 + (t + 1) * P,
                                                     nb * 512:(nb + 1) * 512],
                                          in_=src[:]),
                              waits=later(tf), dma=True, dsem="zs")
                st["z_busy"] = tst

        u_readers = tz
        st["u_busy"] = u_readers
        proj_fm_silu(wg_d, gs_scr, xhat_ready, t0)
        xh0, xh1 = st["w_busy"]
        xhat_busy = xh1 if (xh0 is None or (xh1 and xh1[1] > xh0[1])) else xh0

    prog.barrier()

    qld = [ctx.sb("qld", [P, 1024], F32) for _ in range(2)]
    kld = [ctx.sb("kld", [P, 1024], F32) for _ in range(2)]
    gld = [ctx.sb("gld", [P, 1024], F32) for _ in range(2)]
    vld = [ctx.sb("vld", [P, 2048], F32) for _ in range(2)]
    gst = [ctx.sb("gst", [P, NCHUNK * P], BF16) for _ in range(2)]
    gc_sb = ctx.sb("gc_sb", [P, 1024], F32)
    e_sb = ctx.sb("e_sb", [P, 1024], F32)
    qg_sb = ctx.sb("qg_sb", [P, 1024], F32)
    kg_sb = ctx.sb("kg_sb", [P, 1024], F32)
    kd_sb = ctx.sb("kd_sb", [P, 1024], F32)
    qgf_sb = ctx.sb("qgf_sb", [P, 1024], F32)
    kgf_sb = ctx.sb("kgf_sb", [P, 1024], F32)
    glr_sb = ctx.sb("glr_sb", [1, 1024], F32)
    am_sb = ctx.sb("am_sb", [P, P], F32)
    o2_sb = ctx.sb("o2_sb", [P, 512], F32)
    rr_sb = ctx.sb("rr_sb", [1, P], F32)
    bcr_sb = ctx.sb("bcr_sb", [P, P], F32)
    ec_sb = ctx.sb("ec_sb", [P, 2], F32)
    t1g_sb = ctx.sb("t1g", [P, P], F32)
    og_sb = [ctx.sb("og_sb", [P, P], BF16) for _ in range(2)]
    S_all = ctx.sb("S_all", [P, NH * 2 * 512], F32)

    t_s0 = None
    for h in range(NH * 2):
        t_s0 = prog.op("vector", lambda e, h=h, o=S_all:
                       e.memset(o[:, h * 512:(h + 1) * 512], 0.0))

    busy = {"gc": None, "e": None, "qg": None, "kg": None, "kd": None,
            "qgf": None, "kgf": None, "glr": None, "am": None, "o2": None,
            "rr": None, "bcr": None, "ec": None, "t1g": None,
            "og": [None, None],
            "q": [None, None], "k": [None, None], "g": [None, None],
            "v": [None, None], "gs": [None, None], "S": t_s0}
    pfree = [None] * 6

    for chk in range(nchunks):
        lb = chk % 2
        r0 = chk * P
        tql = prog.op("sync", lambda e, d=qld[lb], r0=r0:
                      e.dma_start(out=d[:], in_=q_scr[r0:r0 + P, :]),
                      waits=later(busy["q"][lb]), dma=True, dsem=f"ql{lb}")
        tkl = prog.op("sync", lambda e, d=kld[lb], r0=r0:
                      e.dma_start(out=d[:], in_=k_scr[r0:r0 + P, :]),
                      waits=later(busy["k"][lb]), dma=True, dsem=f"kl{lb}")
        tgl = prog.op("sync", lambda e, d=gld[lb], r0=r0:
                      e.dma_start(out=d[:], in_=gk_scr[r0:r0 + P, :]),
                      waits=later(busy["g"][lb]), dma=True, dsem=f"gl{lb}")
        tvl = prog.op("sync", lambda e, d=vld[lb], r0=r0:
                      e.dma_start(out=d[:], in_=v_scr[r0:r0 + P, :]),
                      waits=later(busy["v"][lb]), dma=True, dsem=f"vl{lb}")
        tgs = None
        for c in range(NCHUNK):
            tgs = prog.op("sync", lambda e, d=gst[lb], c=c, r0=r0:
                          e.dma_start(out=d[:, c * P:(c + 1) * P],
                                      in_=gs_scr[c * P:(c + 1) * P, r0:r0 + P]),
                          waits=later(busy["gs"][lb]) if c == 0 else (),
                          dma=True, dsem=f"gsl{lb}")

        tgc = [None, None]
        for half in range(2):
            tgc[half] = prog.op("tensor", lambda e, h=half, o=FB[half], g_=gld[lb]:
                                e.matmul(o[:], ut_sb[:],
                                         g_[:, h * 512:(h + 1) * 512],
                                         start=True, stop=True),
                                waits=later(tgl, t_ut, pfree[half]))
        tgce = prog.op("vector", lambda e, o=gc_sb, i0=FB[0], i1=FB[1]:
                       [e.tensor_copy(o[:, 0:512], i0[:]),
                        e.tensor_copy(o[:, 512:1024], i1[:])][-1],
                       waits=later(tgc[0], tgc[1], busy["gc"]))
        pwrite(0, tgce); pwrite(1, tgce)
        tglr = [None, None]
        for half in range(2):
            tglr[half] = prog.op("tensor",
                                 lambda e, h=half, o=FB[2 + half], g_=gld[lb]:
                                 e.matmul(o[0:1, :], onescol[:],
                                          g_[:, h * 512:(h + 1) * 512],
                                          start=True, stop=True),
                                 waits=later(tgl, t_oc, pfree[2 + half]))
        tglre = prog.op("vector", lambda e, o=glr_sb, i0=FB[2], i1=FB[3]:
                        [e.tensor_copy(o[0:1, 0:512], i0[0:1, :]),
                         e.tensor_copy(o[0:1, 512:1024], i1[0:1, :])][-1],
                        waits=later(tglr[0], tglr[1], busy["glr"]))
        pwrite(2, tglre); pwrite(3, tglre)
        tglx = prog.op("scalar", lambda e, o=glr_sb:
                       e.activation(o[:], o[:], AF.Exp), waits=later(tglre))

        teq = prog.op("scalar", lambda e, o=e_sb, i=gc_sb, b=lnS_col:
                      e.activation(o[:], i[:], AF.Exp, bias=b[:]),
                      waits=later(tgce, t_lnS, busy["e"]))
        tqg = prog.op("vector", lambda e, o=qg_sb, a=qld[lb], b=e_sb:
                      e.tensor_tensor(o[:], a[:], b[:], ALU.mult),
                      waits=later(teq, tql, busy["qg"]))
        busy["q"][lb] = tqg
        tek = prog.op("scalar", lambda e, o=e_sb, i=gc_sb:
                      e.activation(o[:], i[:], AF.Exp, scale=-1.0),
                      waits=later(tqg, tgce))
        busy["gc"] = tek
        tkg = prog.op("vector", lambda e, o=kg_sb, a=kld[lb], b=e_sb:
                      e.tensor_tensor(o[:], a[:], b[:], ALU.mult),
                      waits=later(tek, tkl, busy["kg"]))
        busy["k"][lb] = tkg
        busy["e"] = tkg
        tbc = [None, None]
        for half in range(2):
            tbc[half] = prog.op("tensor", lambda e, h=half, o=FB[2 + half]:
                                e.matmul(o[:], ones1[:],
                                         glr_sb[0:1, h * 512:(h + 1) * 512],
                                         start=True, stop=True),
                                waits=later(tglx, t_o1, pfree[2 + half]))
        tkd = prog.op("vector", lambda e, o=kd_sb, a=kg_sb, b0=FB[2], b1=FB[3]:
                      [e.tensor_tensor(o[:, 0:512], a[:, 0:512], b0[:], ALU.mult),
                       e.tensor_tensor(o[:, 512:1024], a[:, 512:1024], b1[:],
                                       ALU.mult)][-1],
                      waits=later(tkg, tbc[0], tbc[1], busy["kd"]))
        pwrite(2, tkd); pwrite(3, tkd)

        tqgf = tkgf = None
        for (src_sb, dst_sb, key, src_tok, bset) in (
                (qg_sb, qgf_sb, "qgf", tqg, (0, 1)),
                (kg_sb, kgf_sb, "kgf", tkg, (2, 3))):
            for rnd in range(2):
                bnk = bset[rnd]
                tmm = None
                for j in range(4):
                    cidx = rnd * 4 + j
                    wl = later(src_tok, t_idf, pfree[bnk],
                               tkd if bset[0] == 2 else None) if j == 0 else ()
                    tmm = prog.op("tensor",
                                  lambda e, b=FB[bnk], j=j, c=cidx, s_=src_sb:
                                  e.transpose(b[:, j * P:(j + 1) * P],
                                              s_[:, c * P:(c + 1) * P],
                                              identf[:]),
                                  waits=wl)
                tcp = prog.op("vector", lambda e, b=FB[bnk], d=dst_sb, rnd=rnd:
                              e.tensor_copy(d[:, rnd * 512:(rnd + 1) * 512], b[:]),
                              waits=later(tmm, busy[key] if rnd == 0 else None))
                pwrite(bnk, tcp)
                if key == "qgf":
                    tqgf = tcp
                else:
                    tkgf = tcp
        busy["qg"] = tqgf
        busy["kg"] = later(tkgf, tkd)[0]
        tr_qg_last = tqgf

        last_S = busy["S"]
        last_o_mm = None
        last_contrib = None
        last_gate = None
        for h in range(NH):
            tA = None
            for dkc in range(2):
                wl = later(tqgf, tkgf, pfree[0]) if dkc == 0 else ()
                tA = prog.op("tensor", lambda e, dkc=dkc, h=h, o=FB[0]:
                             e.matmul(o[:, 0:P],
                                      kgf_sb[:, (2 * h + dkc) * P:
                                             (2 * h + dkc + 1) * P],
                                      qgf_sb[:, (2 * h + dkc) * P:
                                             (2 * h + dkc + 1) * P],
                                      start=(dkc == 0), stop=(dkc == 1)),
                             waits=wl)
            tAm = prog.op("vector", lambda e, o=am_sb, i=FB[0]:
                          e.tensor_tensor(o[:], i[:, 0:P], ut_sb[:], ALU.mult),
                          waits=later(tA, t_ut, busy["am"]))
            pwrite(0, tAm)
            tO = None
            for s_ in range(4):
                for dkc in range(2):
                    wl = ()
                    if s_ == 0 and dkc == 0:
                        wl = later(tqgf, last_S, pfree[4])
                    tO = prog.op("tensor",
                                 lambda e, s_=s_, dkc=dkc, h=h, o=FB[4]:
                                 e.matmul(o[:, s_ * P:(s_ + 1) * P],
                                          S_all[:, (h * 2 + dkc) * 512 + s_ * P:
                                                (h * 2 + dkc) * 512 + (s_ + 1) * P],
                                          qgf_sb[:, (2 * h + dkc) * P:
                                                 (2 * h + dkc + 1) * P],
                                          start=(dkc == 0), stop=False),
                                 waits=wl)
                tO = prog.op("tensor", lambda e, s_=s_, h=h, o=FB[4], v=vld[lb]:
                             e.matmul(o[:, s_ * P:(s_ + 1) * P],
                                      v[:, h * 512 + s_ * P:h * 512 + (s_ + 1) * P],
                                      am_sb[:], start=False, stop=True),
                             waits=later(tAm, tvl))
            last_o_mm = tO
            busy["am"] = tO
            for dkc in range(2):
                tc_ = prog.op("tensor", lambda e, dkc=dkc, h=h, o=FB[2 + dkc],
                              v=vld[lb]:
                              e.matmul(o[:], kd_sb[:, (2 * h + dkc) * P:
                                                  (2 * h + dkc + 1) * P],
                                       v[:, h * 512:(h + 1) * 512],
                                       start=True, stop=True),
                              waits=later(tkd, tvl, pfree[2 + dkc]))
                last_contrib = tc_
                tec = prog.op("tensor", lambda e, dkc=dkc, h=h, o=FB[1]:
                              e.matmul(o[:, dkc:dkc + 1],
                                       glr_sb[0:1, (2 * h + dkc) * P:
                                              (2 * h + dkc + 1) * P],
                                       ones1[0:1, 0:1], start=True, stop=True),
                              waits=later(tglx, t_o1, pfree[1]))
                tece = prog.op("vector", lambda e, dkc=dkc, o=ec_sb, i=FB[1]:
                               e.tensor_copy(o[:, dkc:dkc + 1],
                                             i[:, dkc:dkc + 1]),
                               waits=later(tec, busy["ec"]))
                pwrite(1, tece)
                busy["glr"] = tec
                tS = prog.op("vector", lambda e, dkc=dkc, h=h, o=S_all,
                             c_=FB[2 + dkc], ec=ec_sb:
                             e.scalar_tensor_tensor(
                                 o[:, (h * 2 + dkc) * 512:(h * 2 + dkc + 1) * 512],
                                 o[:, (h * 2 + dkc) * 512:(h * 2 + dkc + 1) * 512],
                                 ec[:, dkc:dkc + 1], c_[:], ALU.mult, ALU.add),
                             waits=later(tc_, tece, tO, last_S))
                busy["ec"] = tS
                pwrite(2 + dkc, tS)
                last_S = tS
            busy["S"] = last_S
            tsq = prog.op("scalar", lambda e, o=o2_sb, i=FB[4]:
                          e.activation(o[:], i[:], AF.Square),
                          waits=later(tO, busy["o2"]))
            trm = None
            for s_ in range(4):
                wl = later(tsq, t_oc, pfree[1]) if s_ == 0 else ()
                trm = prog.op("tensor", lambda e, s_=s_, o=FB[1]:
                              e.matmul(o[0:1, 128:256],
                                       onescol[:], o2_sb[:, s_ * P:(s_ + 1) * P],
                                       start=(s_ == 0), stop=(s_ == 3)),
                              waits=wl)
            trr = prog.op("scalar", lambda e, o=rr_sb, i=FB[1], b=eps_col:
                          e.activation(o[:], i[0:1, 128:256], AF.Ln,
                                       bias=b[0:1, :], scale=1.0 / 512.0),
                          waits=later(trm, t_eps, busy["rr"]))
            pwrite(1, trr)
            trr2 = prog.op("scalar", lambda e, o=rr_sb:
                           e.activation(o[:], o[:], AF.Exp, scale=-0.5),
                           waits=later(trr))
            tbcr = prog.op("tensor", lambda e, o=FB[0]:
                           e.matmul(o[:, 0:P], ones1[:], rr_sb[:],
                                    start=True, stop=True),
                           waits=later(trr2, t_o1, pfree[0]))
            busy["rr"] = tbcr
            tbcre = prog.op("vector", lambda e, o=bcr_sb, i=FB[0]:
                            e.tensor_copy(o[:], i[:, 0:P]),
                            waits=later(tbcr, busy["bcr"]))
            pwrite(0, tbcre)
            tg2 = None
            for s_ in range(4):
                tg1 = prog.op("vector", lambda e, s_=s_, o=t1g_sb, i=FB[4],
                              g_=gam_sb:
                              e.scalar_tensor_tensor(o[:], i[:, s_ * P:(s_ + 1) * P],
                                                     g_[:, s_:s_ + 1], bcr_sb[:],
                                                     ALU.mult, ALU.mult),
                              waits=later(tbcre, tsq, t_gam, busy["t1g"]))
                busy["bcr"] = tg1
                ob = s_ % 2
                tg2 = prog.op("vector", lambda e, s_=s_, h=h, o=og_sb[ob], a=t1g_sb,
                              g_=gst[lb]:
                              e.tensor_tensor(o[:], a[:],
                                              g_[:, (h * 4 + s_) * P:
                                                 (h * 4 + s_ + 1) * P], ALU.mult),
                              waits=later(tg1, tgs, busy["og"][ob]))
                busy["t1g"] = tg2
                tst = prog.op("sync", lambda e, s_=s_, h=h, chk=chk, src=og_sb[ob]:
                              e.dma_start(out=og_scr[chk].rearrange(
                                  "p (c w) -> p c w", w=P)[:, h * 4 + s_, :],
                                  in_=src[:]),
                              waits=later(tg2), dma=True, dsem=f"og{ob}")
                busy["og"][ob] = tst
                if s_ == 3:
                    pwrite(4, tg1)
            busy["o2"] = trm
            last_gate = tg2
        busy["g"][lb] = later(tgc[0], tgc[1], tglr[0], tglr[1])[0]
        busy["v"][lb] = later(last_o_mm, last_contrib)[0]
        busy["gs"][lb] = last_gate

    prog.barrier()

    og_extra = [ctx.sb("op_act", [P, NCHUNK * P], BF16) for _ in range(2)]
    emit_out_proj(ctx, og_scr, wo_d, x_d, y_d, T, NCHUNK,
                  [FB[0], FB[1], FB[2], FB[3]], x_sb=xbufs,
                  act_sb=[gst[0], gst[1]] + og_extra, wt=wtile, ysb=evb)

    for s_, v in list(prog.cnt.items()):
        prog.wait("sync", s_, v)
    prog.build()
    return nc


def build_moe(T):
    I = 1408
    NI = I // P
    SI = 2816
    NSI = SI // P
    E = 8
    ngroups = T // 512

    nc = bass.Bass("TRN2", target_bir_lowering=False)
    x_d = nc.dram_tensor("x", [T, D], F32, kind="ExternalInput").ap()
    rw_d = nc.dram_tensor("rwt", [P, NCHUNK * E], F32, kind="ExternalInput").ap()
    gw_d = nc.dram_tensor("gw", [E, NI, P, NCHUNK, P], F32, kind="ExternalInput").ap()
    uw_d = nc.dram_tensor("uw", [E, NI, P, NCHUNK, P], F32, kind="ExternalInput").ap()
    dw_d = nc.dram_tensor("dw", [E, I, D], F32, kind="ExternalInput").ap()
    sg_d = nc.dram_tensor("sg", [NSI, P, NCHUNK, P], F32, kind="ExternalInput").ap()
    su_d = nc.dram_tensor("su", [NSI, P, NCHUNK, P], F32, kind="ExternalInput").ap()
    sd_d = nc.dram_tensor("sd", [SI, D], F32, kind="ExternalInput").ap()
    sgb_d = nc.dram_tensor("sgb2", [P, NSI], F32, kind="ExternalInput").ap()
    sub_d = nc.dram_tensor("sub2", [P, NSI], F32, kind="ExternalInput").ap()
    sdb_d = nc.dram_tensor("sdb_row", [1, D], F32, kind="ExternalInput").ap()
    id_d = nc.dram_tensor("ident", [P, P], F32, kind="ExternalInput").ap()
    y_d = nc.dram_tensor("y", [T, D], F32, kind="ExternalOutput").ap()

    ctx = Ctx(nc)
    prog = ctx.prog

    ident, t_id = emit_const_load(ctx, id_d, [P, P], BF16, "ident")
    rw_sb, t_rw = emit_const_load(ctx, rw_d, [P, NCHUNK * E], BF16, "rw")
    sgb_sb, t_sgb = emit_const_load(ctx, sgb_d, [P, NSI], F32, "sgb")
    sub_sb, t_sub = emit_const_load(ctx, sub_d, [P, NSI], F32, "sub")
    sdb_row, t_sdb = emit_const_load(ctx, sdb_d, [1, D], F32, "sdbr")
    lnb = make_ln_bufs(ctx, 512)
    lnb["ident"] = ident
    lnb["prev_xhat_reader"] = t_id

    ones1 = ctx.sb("ones1", [1, P], F32)
    t_o1 = prog.op("vector", lambda e, t=ones1: e.memset(t[:], 1.0))

    xhat_fm = ctx.sb("xhat_fm", [P, NCHUNK * 512], BF16)
    xbufs = [ctx.sb("xk", [P, D], F32) for _ in range(4)]
    wt = [ctx.sb("wt", [P, NCHUNK * P], BF16) for _ in range(2)]
    wt2 = [ctx.sb("wt2", [P, NCHUNK * P], BF16) for _ in range(2)]
    hgate = [ctx.sb("hgate", [P, 512], F32) for _ in range(2)]
    hup = [ctx.sb("hup", [P, 512], F32) for _ in range(2)]
    h_fm = ctx.sb("h_fm", [P, NSI * 512], BF16)
    acc = [ctx.sb("acc", [P, D], F32) for _ in range(4)]
    ysb = [ctx.sb("ysb", [P, 512], F32) for _ in range(2)]
    lg = ctx.sb("lg", [P, E], F32)
    rmax = ctx.sb("rmax", [P, 1], F32)
    p2c = ctx.sb("p2c", [P, 1], F32)
    msk = ctx.sb("msk", [P, E], F32)
    ce_all = ctx.sb("ce_all", [P, 4 * E], F32)
    sdb_bc = ctx.sb("sdb_bc", [P, D], F32)

    pg = ctx.ps("pg", [P, 512])
    pu = ctx.ps("pu", [P, 512])
    pdn = [ctx.ps("pdn", [P, 512]) for _ in range(4)]

    tprev = None
    for blk in range(4):
        tm = prog.op("tensor", lambda e, b=blk, o=pg:
                     e.matmul(o[:], ones1[:], sdb_row[0:1, b * 512:(b + 1) * 512],
                              start=True, stop=True),
                     waits=later(t_o1, t_sdb, tprev))
        tprev = prog.op("vector", lambda e, b=blk, o=sdb_bc, i=pg:
                        e.tensor_copy(o[:, b * 512:(b + 1) * 512], i[:]),
                        waits=later(tm))
    pg_busy = tprev

    xhat_busy = None
    x_busy4 = [None] * 4
    w_busy = [None, None]
    w2_busy = [None, None]
    hg_busy = [None, None]
    hu_busy = [None, None]
    pu_busy = None
    pdn_busy = [None] * 4
    h_busy = None
    acc_busy = [None] * 4
    y_busy = [None, None]
    lg_busy = None
    ce_busy = [None] * 4

    for g in range(ngroups):
        t_fm = None
        for t in range(4):
            guards = [x_busy4[t]]
            if t == 0:
                guards.append(xhat_busy)
            lnb["xsem"] = f"xk{t}"
            _, t_fm = emit_ln_xhat(ctx, x_d, g * 4 + t, lnb, xhat_fm, t * P,
                                   xbufs[t], x_write_guards=guards)
            x_busy4[t] = t_fm
        xhat_ready = t_fm

        for t in range(4):
            tmm = None
            for c in range(NCHUNK):
                wl = later(t_rw, xhat_ready, pg_busy, *(
                    [ce_busy[t]] if c == 0 else [])) if c == 0 else ()
                tmm = prog.op("tensor", lambda e, t=t, c=c, o=pg:
                              e.matmul(o[:, 0:E],
                                       xhat_fm[:, c * 512 + t * P:
                                               c * 512 + (t + 1) * P],
                                       rw_sb[:, c * E:(c + 1) * E],
                                       start=(c == 0), stop=(c == NCHUNK - 1)),
                              waits=wl)
            tv0 = prog.op("vector", lambda e, o=lg, i=pg:
                          e.tensor_copy(o[:], i[:, 0:E]),
                          waits=later(tmm, lg_busy))
            pg_busy = tv0
            tv1 = prog.op("vector", lambda e, o=rmax, i=lg:
                          e.reduce_max(o[:], i[:], axis=AX.X), waits=later(tv0))
            tv2 = prog.op("vector", lambda e, o=lg, m=rmax:
                          e.tensor_scalar(o[:], o[:], m[:], None, ALU.subtract),
                          waits=later(tv1))
            tv3 = prog.op("scalar", lambda e, o=lg:
                          e.activation(o[:], o[:], AF.Exp), waits=later(tv2))
            tv4 = prog.op("vector", lambda e, o=rmax, i=lg:
                          e.reduce_max(o[:], i[:], axis=AX.X), waits=later(tv3))
            tv5 = prog.op("vector", lambda e, o=msk, i=lg, m=rmax:
                          e.tensor_scalar(o[:], i[:], m[:], None, ALU.is_ge),
                          waits=later(tv4))
            tv6 = prog.op("vector", lambda e, o=msk, i=lg:
                          e.scalar_tensor_tensor(o[:], o[:], -1e30, i[:],
                                                 ALU.mult, ALU.add),
                          waits=later(tv5))
            tv7 = prog.op("vector", lambda e, o=p2c, i=msk:
                          e.reduce_max(o[:], i[:], axis=AX.X), waits=later(tv6))
            tv8 = prog.op("vector", lambda e, o=msk, i=lg, m=p2c:
                          e.tensor_scalar(o[:], i[:], m[:], None, ALU.is_ge),
                          waits=later(tv7))
            tv9 = prog.op("vector", lambda e, o=msk, i=lg:
                          e.tensor_tensor(o[:], o[:], i[:], ALU.mult),
                          waits=later(tv8))
            tva = prog.op("vector", lambda e, o=p2c, a=rmax:
                          e.tensor_tensor(o[:], o[:], a[:], ALU.add),
                          waits=later(tv9))
            tvb = prog.op("vector", lambda e, o=p2c:
                          e.reciprocal(o[:], o[:]), waits=later(tva))
            tvc = prog.op("vector", lambda e, o=ce_all, t=t, i=msk, m=p2c:
                          e.tensor_scalar(o[:, t * E:(t + 1) * E], i[:], m[:],
                                          None, ALU.mult),
                          waits=later(tvb, ce_busy[t]))
            ce_busy[t] = tvc
            lg_busy = tvc

        acc_init = []
        for t in range(4):
            ti = prog.op("vector", lambda e, o=acc[t]:
                         e.tensor_copy(o[:], sdb_bc[:]),
                         waits=later(tprev, acc_busy[t]))
            acc_init.append(ti)
            acc_busy[t] = ti

        def gu_pass(ni, gsrc, usrc, bias_g, bias_u, is_expert, eidx):
            nonlocal pu_busy, pg_busy, h_busy
            th = None
            for i in range(ni):
                bw = i % 2
                if is_expert:
                    sg_ = gsrc[eidx][i]
                    su_ = usrc[eidx][i]
                    tg = prog.op("gpsimd", lambda e, d=wt[bw], s=sg_:
                                 e.dma_start(out=d[:, 0:NCHUNK * P],
                                             in_=s.rearrange("p c o -> p (c o)")),
                                 waits=later(w_busy[bw]), dma=True, dsem=f"wg{bw}")
                    tu = prog.op("gpsimd", lambda e, d=wt2[bw], s=su_:
                                 e.dma_start(out=d[:, 0:NCHUNK * P],
                                             in_=s.rearrange("p c o -> p (c o)")),
                                 waits=later(w2_busy[bw]), dma=True, dsem=f"wu{bw}")
                else:
                    tg = prog.op("gpsimd", lambda e, d=wt[bw], s=gsrc[i]:
                                 e.dma_start(out=d[:, 0:NCHUNK * P],
                                             in_=s.rearrange("p c o -> p (c o)")),
                                 waits=later(w_busy[bw]), dma=True, dsem=f"wg{bw}")
                    tu = prog.op("gpsimd", lambda e, d=wt2[bw], s=usrc[i]:
                                 e.dma_start(out=d[:, 0:NCHUNK * P],
                                             in_=s.rearrange("p c o -> p (c o)")),
                                 waits=later(w2_busy[bw]), dma=True, dsem=f"wu{bw}")
                tmg = tmu = None
                for c in range(NCHUNK):
                    wl = later(tg, xhat_ready, pg_busy) if c == 0 else ()
                    tmg = prog.op("tensor", lambda e, w=wt[bw], c=c, o=pg:
                                  e.matmul(o[:], w[:, c * P:(c + 1) * P],
                                           xhat_fm[:, c * 512:(c + 1) * 512],
                                           start=(c == 0), stop=(c == NCHUNK - 1)),
                                  waits=wl)
                for c in range(NCHUNK):
                    wl = later(tu, pu_busy) if c == 0 else ()
                    tmu = prog.op("tensor", lambda e, w=wt2[bw], c=c, o=pu:
                                  e.matmul(o[:], w[:, c * P:(c + 1) * P],
                                           xhat_fm[:, c * 512:(c + 1) * 512],
                                           start=(c == 0), stop=(c == NCHUNK - 1)),
                                  waits=wl)
                w_busy[bw] = tmg
                w2_busy[bw] = tmu
                hb = i % 2
                if bias_g is not None:
                    ts_ = prog.op("scalar", lambda e, o=hgate[hb], i_=pg, b=bias_g,
                                  i=i:
                                  e.activation(o[:], i_[:], AF.Silu,
                                               bias=b[:, i:i + 1]),
                                  waits=later(tmg, hg_busy[hb], t_sgb))
                else:
                    ts_ = prog.op("scalar", lambda e, o=hgate[hb], i_=pg:
                                  e.activation(o[:], i_[:], AF.Silu),
                                  waits=later(tmg, hg_busy[hb]))
                pg_busy = ts_
                if bias_u is not None:
                    tsu = prog.op("vector", lambda e, o=hup[hb], i_=pu, b=bias_u,
                                  i=i:
                                  e.tensor_scalar(o[:], i_[:], b[:, i:i + 1], None,
                                                  ALU.add),
                                  waits=later(tmu, hu_busy[hb], t_sub))
                    pu_busy = tsu
                    th = prog.op("vector", lambda e, hg=hgate[hb], up=hup[hb],
                                 o=h_fm, i=i:
                                 e.tensor_tensor(o[:, i * 512:(i + 1) * 512],
                                                 hg[:], up[:], ALU.mult),
                                 waits=later(ts_, tsu,
                                             h_busy if i == 0 else None))
                    hu_busy[hb] = th
                else:
                    th = prog.op("vector", lambda e, hg=hgate[hb], up=pu, o=h_fm,
                                 i=i:
                                 e.tensor_tensor(o[:, i * 512:(i + 1) * 512],
                                                 hg[:], up[:], ALU.mult),
                                 waits=later(ts_, tmu,
                                             h_busy if i == 0 else None))
                    pu_busy = th
                hg_busy[hb] = th
            return th

        def down_pass(ni, dsrc, is_expert, eidx, final):
            nonlocal h_busy, y_busy
            h_ready = None
            for n in range(4):
                last_mm = None
                for i in range(ni):
                    bw = i % 2
                    if is_expert:
                        s_ = dsrc[eidx][i * P:(i + 1) * P, n * 512:(n + 1) * 512]
                    else:
                        s_ = dsrc[i * P:(i + 1) * P, n * 512:(n + 1) * 512]
                    td = prog.op("gpsimd", lambda e, d=wt[bw], s=s_:
                                 e.dma_start(out=d[:, 0:512], in_=s),
                                 waits=later(w_busy[bw]), dma=True, dsem=f"wg{bw}")
                    for t in range(4):
                        wl = ()
                        if t == 0:
                            wl = later(td, h_ready_tok,
                                       *(pdn_busy if i == 0 else []))
                        last_mm = prog.op("tensor",
                                          lambda e, w=wt[bw], t=t, i=i, o=pdn[t]:
                                          e.matmul(o[:],
                                                   h_fm[:, i * 512 + t * P:
                                                        i * 512 + (t + 1) * P],
                                                   w[:, 0:512], start=(i == 0),
                                                   stop=(i == ni - 1)),
                                          waits=wl)
                    w_busy[bw] = last_mm
                for t in range(4):
                    if not final:
                        tev = prog.op("vector",
                                      lambda e, t=t, n=n, ps=pdn[t], o=acc[t],
                                      ce=ce_all, ei=eidx:
                                      e.scalar_tensor_tensor(
                                          o[:, n * 512:(n + 1) * 512], ps[:],
                                          ce[:, t * E + ei:t * E + ei + 1],
                                          o[:, n * 512:(n + 1) * 512],
                                          ALU.mult, ALU.add),
                                      waits=later(last_mm, acc_busy[t],
                                                  ce_busy[t]))
                        acc_busy[t] = tev
                        pdn_busy[t] = tev
                    else:
                        k = (n * 4 + t) % 2
                        tev = prog.op("vector",
                                      lambda e, t=t, n=n, ps=pdn[t], o=ysb[k],
                                      a=acc[t]:
                                      e.tensor_tensor(o[:], ps[:],
                                                      a[:, n * 512:(n + 1) * 512],
                                                      ALU.add),
                                      waits=later(last_mm, acc_busy[t],
                                                  y_busy[k]))
                        tev = prog.op("vector",
                                      lambda e, o=ysb[k], x=xbufs[t], n=n:
                                      e.tensor_tensor(o[:], o[:],
                                                      x[:, n * 512:(n + 1) * 512],
                                                      ALU.add),
                                      waits=later(tev))
                        pdn_busy[t] = tev
                        acc_busy[t] = tev
                        if n == 3:
                            x_busy4[t] = tev
                        tst = prog.op("sync",
                                      lambda e, yb=ysb[k], g=g, t=t, n=n:
                                      e.dma_start(
                                          out=y_d[(g * 4 + t) * P:
                                                  (g * 4 + t + 1) * P,
                                                  n * 512:(n + 1) * 512],
                                          in_=yb[:]),
                                      waits=later(tev), dma=True, dsem=f"ys{k}")
                        y_busy[k] = tst
            h_busy = last_mm
            return last_mm

        for eidx in range(E):
            h_ready_tok = gu_pass(NI, gw_d, uw_d, None, None, True, eidx)
            down_pass(NI, dw_d, True, eidx, False)
        for t in range(4):
            ce_busy[t] = acc_busy[t]
        h_ready_tok = gu_pass(NSI, sg_d, su_d, sgb_sb, sub_sb, False, 0)
        down_pass(NSI, sd_d, False, 0, True)
        xhat_busy = w_busy[0] if (w_busy[1] is None or
                                  (w_busy[0] and w_busy[0][1] > w_busy[1][1])) \
            else w_busy[1]

    for s_, v in list(prog.cnt.items()):
        prog.wait("sync", s_, v)
    prog.build()
    return nc


def _np(x):
    return np.ascontiguousarray(np.asarray(x, dtype=np.float32))


IDENT = np.eye(P, dtype=np.float32)


def tile_w_fm(w):
    din, dout = w.shape
    t = w.reshape(din // P, P, dout // P, P)
    return _np(t.transpose(2, 1, 0, 3))


def run_ffn_layer(x_flat, w_gate_up, w_down, nc_cache={}):
    INTER = 8192
    key = ("ffn", x_flat.shape[0] // 8)
    if key not in nc_cache:
        nc_cache[key] = build_ffn(x_flat.shape[0] // 8)
    nc = nc_cache[key]
    wg = tile_w_fm(_np(w_gate_up[:INTER]).T)
    wu = tile_w_fm(_np(w_gate_up[INTER:]).T)
    wd = _np(w_down).T.copy()
    Tc = x_flat.shape[0] // 8
    in_maps = [{"x": _np(x_flat[c * Tc:(c + 1) * Tc]),
                "wg": wg, "wu": wu, "wd": _np(wd), "ident": IDENT}
               for c in range(8)]
    res = run_bass_kernel_spmd(nc, in_maps, core_ids=list(range(8)))
    return np.concatenate([res.results[c]["y"] for c in range(8)], axis=0)


def make_sel(dh):
    hpc = P // dh
    sel = np.zeros((P, hpc), dtype=np.float32)
    for j in range(hpc):
        sel[j * dh:(j + 1) * dh, j] = 1.0
    return sel


def make_dmask():
    r = np.arange(P)[:, None]
    q = np.arange(512)[None, :]
    return np.stack([(r + 128 * rr <= q).astype(np.float32) for rr in range(4)])


def run_attn_layer(x_seqs, cfg, weights, nc_cache={}):
    H, G, dh, has_bias = cfg
    T = 4096
    key = ("attn", H, G, dh, has_bias)
    if key not in nc_cache:
        nc_cache[key] = build_attn(T, H, G, dh, has_bias)
    nc = nc_cache[key]
    base = {"wq": _np(weights["wq"]), "wk": _np(weights["wk"]),
            "wv": _np(weights["wv"]), "wo": _np(weights["wo"]),
            "ident": IDENT, "sel": make_sel(dh), "dmask": make_dmask()}
    if has_bias:
        NMQ = H * dh // P
        NMK = G * dh // P
        base["bq2"] = _np(weights["bq"]).reshape(NMQ, P).T.copy()
        base["bk2"] = _np(weights["bk"]).reshape(NMK, P).T.copy()
        base["bv_row"] = _np(weights["bv"]).reshape(1, -1)
        base["bo_row"] = _np(weights["bo"]).reshape(1, -1)
    in_maps = [dict(base, x=_np(x_seqs[c // 2])) for c in range(8)]
    res = run_bass_kernel_spmd(nc, in_maps, core_ids=list(range(8)))
    out = np.empty((4, 4096, D), np.float32)
    for c in range(8):
        half = c % 2
        out[c // 2, half * 2048:(half + 1) * 2048] = \
            res.results[c]["y"][half * 2048:(half + 1) * 2048]
    return out


def run_gla_layer(x_seqs, weights, nc_cache={}):
    T = 4096
    key = ("gla",)
    if key not in nc_cache:
        nc_cache[key] = build_gla(T)
    nc = nc_cache[key]
    wgk1t = _np(weights["Wgk1"]).reshape(NCHUNK, P, 16).transpose(1, 0, 2)
    base = {"wq": _np(weights["Wq"]), "wk": _np(weights["Wk"]),
            "wv": _np(weights["Wv"]), "wg": _np(weights["Wg"]),
            "wo": _np(weights["Wo"]), "wgk1t": _np(wgk1t).reshape(P, NCHUNK * 16),
            "wgk2": _np(weights["Wgk2"]),
            "bgk2_row": _np(weights["bgk2"]).reshape(1, 1024),
            "gamma2": _np(weights["g_norm_weight"]).reshape(4, P).T.copy(),
            "ident": IDENT, "ut": np.triu(np.ones((P, P), np.float32))}
    in_maps = [dict(base, x=_np(x_seqs[c // 2])) for c in range(8)]
    res = run_bass_kernel_spmd(nc, in_maps, core_ids=list(range(8)))
    out = np.empty((4, 4096, D), np.float32)
    for c in range(8):
        half = c % 2
        out[c // 2, half * 2048:(half + 1) * 2048] = \
            res.results[c]["y"][half * 2048:(half + 1) * 2048]
    return out


def run_moe_layer(x_flat, W, nc_cache={}):
    Tc = x_flat.shape[0] // 8
    key = ("moe", Tc)
    if key not in nc_cache:
        nc_cache[key] = build_moe(Tc)
    nc = nc_cache[key]
    NI, NSI, E = 11, 22, 8
    rwt = _np(W["router_w"]).T.reshape(NCHUNK, P, E).transpose(1, 0, 2)
    gw = _np(W["gate_w"]).reshape(E, NCHUNK, P, NI, P).transpose(0, 3, 2, 1, 4)
    uw = _np(W["up_w"]).reshape(E, NCHUNK, P, NI, P).transpose(0, 3, 2, 1, 4)
    sg = _np(W["s_gate_w"]).reshape(NCHUNK, P, NSI, P).transpose(2, 1, 0, 3)
    su = _np(W["s_up_w"]).reshape(NCHUNK, P, NSI, P).transpose(2, 1, 0, 3)
    base = {"rwt": _np(rwt).reshape(P, NCHUNK * E), "gw": _np(gw), "uw": _np(uw),
            "dw": _np(W["down_w"]), "sg": _np(sg), "su": _np(su),
            "sd": _np(W["s_down_w"]),
            "sgb2": _np(W["s_gate_b"]).reshape(NSI, P).T.copy(),
            "sub2": _np(W["s_up_b"]).reshape(NSI, P).T.copy(),
            "sdb_row": _np(W["s_down_b"]).reshape(1, D), "ident": IDENT}
    in_maps = [dict(base, x=_np(x_flat[c * Tc:(c + 1) * Tc])) for c in range(8)]
    res = run_bass_kernel_spmd(nc, in_maps, core_ids=list(range(8)))
    return np.concatenate([res.results[c]["y"] for c in range(8)], axis=0)


def _aot_compile(nc, n_cores=8):
    import jax
    from concourse import bass2jax
    from jax.sharding import Mesh, PartitionSpec
    from jax.experimental.shard_map import shard_map

    bass2jax.install_neuronx_cc_hook()
    partition_name = nc.partition_id_tensor.name if nc.partition_id_tensor else None
    in_names, out_names, out_avals = [], [], []
    zero_shapes = []
    in_shapes = {}
    for alloc in nc.m.functions[0].allocations:
        if not isinstance(alloc, mybir.MemoryLocationSet):
            continue
        name = alloc.memorylocations[0].name
        if alloc.kind == "ExternalInput":
            in_shapes[name] = (tuple(alloc.tensor_shape), mybir.dt.np(alloc.dtype))
            if name != partition_name:
                in_names.append(name)
        elif alloc.kind == "ExternalOutput":
            shape = tuple(alloc.tensor_shape)
            dtype = mybir.dt.np(alloc.dtype)
            out_names.append(name)
            out_avals.append(jax.core.ShapedArray(shape, dtype))
            zero_shapes.append((shape, dtype))
    n_params = len(in_names)
    all_names = list(in_names) + list(out_names)
    if partition_name is not None:
        all_names.append(partition_name)

    def _body(*args):
        operands = list(args)
        if partition_name is not None:
            operands.append(bass2jax.partition_id_tensor())
        outs = bass2jax._bass_exec_p.bind(
            *operands, out_avals=tuple(out_avals), in_names=tuple(all_names),
            out_names=tuple(out_names), lowering_input_output_aliases=(),
            sim_require_finite=True, sim_require_nnan=True, nc=nc)
        return tuple(outs)

    devices = jax.devices()[:n_cores]
    mesh = Mesh(np.asarray(devices), ("core",))
    n_outs = len(out_avals)
    in_specs = (PartitionSpec("core"),) * (n_params + n_outs)
    out_specs = (PartitionSpec("core"),) * n_outs
    donate = tuple(range(n_params, n_params + n_outs))
    f = jax.jit(
        shard_map(_body, mesh=mesh, in_specs=in_specs, out_specs=out_specs,
                  check_rep=False),
        donate_argnums=donate, keep_unused=True)
    args = []
    for nm in in_names:
        shp, dt = in_shapes[nm]
        args.append(jax.ShapeDtypeStruct((n_cores * shp[0],) + tuple(shp[1:]), dt))
    for shp, dt in zero_shapes:
        args.append(jax.ShapeDtypeStruct((n_cores * shp[0],) + tuple(shp[1:]), dt))
    f.lower(*args).compile()


def _precompile_all(ncs):
    import concurrent.futures as cf

    def one(nc):
        try:
            _aot_compile(nc)
        except Exception:
            pass

    with cf.ThreadPoolExecutor(max_workers=6) as ex:
        list(ex.map(one, ncs))


def kernel(**inputs):
    g = lambda k: _np(inputs[k])
    x4 = g("L0_attn_a_hidden_states")

    gla_w = {"Wq": g("L0_attn_a_Wq"), "Wk": g("L0_attn_a_Wk"),
             "Wv": g("L0_attn_a_Wv"), "Wgk1": g("L0_attn_a_Wgk1"),
             "Wgk2": g("L0_attn_a_Wgk2"), "bgk2": g("L0_attn_a_bgk2"),
             "Wg": g("L0_attn_a_Wg"), "g_norm_weight": g("L0_attn_a_g_norm_weight"),
             "Wo": g("L0_attn_a_Wo")}
    l0b_w = {"wq": g("L0_attn_b_Wq"), "wk": g("L0_attn_b_Wk"),
             "wv": g("L0_attn_b_Wv"), "wo": g("L0_attn_b_Wo")}
    ffn_gu = g("L0_ffn_w_gate_up")
    ffn_d = g("L0_ffn_w_down")
    l1a_w = {"wq": _np(g("L1_attn_a_Wq").T), "wk": _np(g("L1_attn_a_Wk").T),
             "wv": _np(g("L1_attn_a_Wv").T), "wo": _np(g("L1_attn_a_Wo").T),
             "bq": g("L1_attn_a_bq"), "bk": g("L1_attn_a_bk"),
             "bv": g("L1_attn_a_bv"), "bo": g("L1_attn_a_bo")}
    wqkvT = _np(g("L1_attn_b_w_qkv").T)
    l1b_w = {"wq": _np(wqkvT[:, 0:2048]), "wk": _np(wqkvT[:, 2048:4096]),
             "wv": _np(wqkvT[:, 4096:6144]), "wo": _np(g("L1_attn_b_w_o").T)}
    moe_w = {"router_w": g("L1_ffn_router_w"), "gate_w": g("L1_ffn_gate_w"),
             "up_w": g("L1_ffn_up_w"), "down_w": g("L1_ffn_down_w"),
             "s_gate_w": g("L1_ffn_s_gate_w"), "s_gate_b": g("L1_ffn_s_gate_b"),
             "s_up_w": g("L1_ffn_s_up_w"), "s_up_b": g("L1_ffn_s_up_b"),
             "s_down_w": g("L1_ffn_s_down_w"), "s_down_b": g("L1_ffn_s_down_b")}

    import inspect
    gcache = inspect.signature(run_gla_layer).parameters["nc_cache"].default
    acache = inspect.signature(run_attn_layer).parameters["nc_cache"].default
    fcache = inspect.signature(run_ffn_layer).parameters["nc_cache"].default
    mcache = inspect.signature(run_moe_layer).parameters["nc_cache"].default
    T4 = 4096
    if ("gla",) not in gcache:
        gcache[("gla",)] = build_gla(T4)
    for cfg in ((32, 8, 64, False), (16, 16, 128, True), (16, 16, 128, False)):
        key = ("attn",) + cfg
        if key not in acache:
            acache[key] = build_attn(T4, *cfg)
    if ("ffn", 2048) not in fcache:
        fcache[("ffn", 2048)] = build_ffn(2048)
    if ("moe", 2048) not in mcache:
        mcache[("moe", 2048)] = build_moe(2048)
    _precompile_all([gcache[("gla",)],
                     acache[("attn", 32, 8, 64, False)],
                     acache[("attn", 16, 16, 128, True)],
                     acache[("attn", 16, 16, 128, False)],
                     fcache[("ffn", 2048)],
                     mcache[("moe", 2048)]])

    x4 = run_gla_layer(x4, gla_w)
    xf = run_ffn_layer(x4.reshape(-1, D), ffn_gu, ffn_d)
    x4 = run_attn_layer(xf.reshape(4, 4096, D), (32, 8, 64, False), l0b_w)
    xf = run_ffn_layer(x4.reshape(-1, D), ffn_gu, ffn_d)
    x4 = run_attn_layer(xf.reshape(4, 4096, D), (16, 16, 128, True), l1a_w)
    xf = run_moe_layer(x4.reshape(-1, D), moe_w)
    x4 = run_attn_layer(xf.reshape(4, 4096, D), (16, 16, 128, False), l1b_w)
    xf = run_moe_layer(x4.reshape(-1, D), moe_w)
    return xf.reshape(4, 4096, D)
